# version 26
# baseline (speedup 1.0000x reference)
import math
import traceback
from contextlib import ExitStack

DEBUG_LOC = True

import numpy as np
import ml_dtypes

import concourse.bass as bass
import concourse.mybir as mybir
from concourse.bass_utils import run_bass_kernel_spmd

F32 = mybir.dt.float32
BF16 = mybir.dt.bfloat16
I32 = mybir.dt.int32
AF = mybir.ActivationFunctionType
ALU = mybir.AluOpType
AX = mybir.AxisListType

T = 2048
DM = 2048
NH = 16
HD = 128
NT = 16
NCH = 16
I0 = 5504
I1 = 5632
NI0 = I0 // 128
NI1 = I1 // 128
SLACK = 72
SCALE = 1.0 / math.sqrt(HD)
NBIS = 38

ENGS = ("sync", "scalar", "vector", "tensor", "gpsimd")


class Sem:
    __slots__ = ("h", "n")
    def __init__(self, h):
        self.h = h
        self.n = 0


class Prog:

    def __init__(self, nc):
        self.nc = nc
        self.ops = {e: [] for e in ENGS}
        self.sems = {}

    def sem(self, name):
        if name not in self.sems:
            self.sems[name] = Sem(self.nc.alloc_semaphore(name))
        return self.sems[name]

    def n(self, name):
        return self.sem(name).n

    def op(self, eng, meth, *args, inc=None, inck=1, **kw):
        loc = traceback.extract_stack(limit=2)[0] if DEBUG_LOC else None
        if inc is not None:
            s = self.sem(inc)
            s.n += inck
            def fn(e, meth=meth, args=args, kw=kw, s=s, inck=inck, loc=loc):
                try:
                    getattr(e, meth)(*args, **kw).then_inc(s.h, inck)
                except Exception:
                    print(f"FAILED OP {meth} recorded at {loc.filename}:{loc.lineno}")
                    raise
            self.ops[eng].append(fn)
            return s.n
        def fn(e, meth=meth, args=args, kw=kw, loc=loc):
            try:
                getattr(e, meth)(*args, **kw)
            except Exception:
                print(f"FAILED OP {meth} recorded at {loc.filename}:{loc.lineno}")
                raise
        self.ops[eng].append(fn)
        return None

    def wait(self, eng, semname, val=None):
        s = self.sem(semname)
        if val is None:
            val = s.n
        if val <= 0:
            return
        def fn(e, s=s, val=val):
            e.wait_ge(s.h, val)
        self.ops[eng].append(fn)

    def dma(self, queue, out, in_, stream="m", **kw):
        semname = {"sync": "dS", "scalar": "dA", "gpsimd": "dG"}[queue] + "_" + stream
        return self.op(queue, "dma_start", inc=semname, inck=16, out=out, in_=in_, **kw)

    BG_STREAMS = ("dG_cv8",)

    def barrier(self):
        for n in list(self.sems):
            if (n.startswith("d") and "_" in n and self.sems[n].n > 0
                    and n not in self.BG_STREAMS):
                self.wait("gpsimd", n, self.sems[n].n)
        s = self.sem("bar")
        s.n += len(ENGS)
        target = s.n
        for e in ENGS:
            def fin(eng, s=s):
                eng.nop().then_inc(s.h, 1)
            self.ops[e].append(fin)
        for e in ENGS:
            def fw(eng, s=s, target=target):
                eng.wait_ge(s.h, target)
            self.ops[e].append(fw)

    def replay(self, block):
        for ename in ENGS:
            deco = getattr(block, ename)
            oplist = self.ops[ename]
            def section(e, oplist=oplist):
                for fn in oplist:
                    fn(e)
            deco(section)


HBM_SPECS = {
    "x":      ((T, DM), F32, "in"),
    "wq":     ((NCH, 128, DM), F32, "in"),
    "wk":     ((NCH, 128, DM), F32, "in"),
    "wv":     ((NCH, 128, DM), F32, "in"),
    "bq":     ((128, NCH), F32, "in"),
    "bk":     ((128, NCH), F32, "in"),
    "bv":     ((128, NCH), F32, "in"),
    "lut":    ((1, 16), F32, "in"),
    "gidxT":  ((NI0, 128, DM), I32, "in"),
    "uidxT":  ((NI0, 128, DM), I32, "in"),
    "didxT":  ((I0, DM), I32, "in"),
    "wqkv":   ((3, NH, 128, DM), F32, "in"),
    "wout":   ((4, NCH, 128, 512), F32, "in"),
    "decay":  ((1, NH), F32, "in"),
    "gwT":    ((NI1, 128, DM), F32, "in"),
    "uwT":    ((NI1, 128, DM), F32, "in"),
    "dwT":    ((NI1, 128, DM), F32, "in"),
    "gw8":    ((NI1, 128, DM), BF16, "scratch"),
    "uw8":    ((NI1, 128, DM), BF16, "scratch"),
    "dw8":    ((NI1, 128, DM), BF16, "scratch"),
    "wqkv8":  ((3, NH, 128, DM), BF16, "scratch"),
    "wout8":  ((4, NCH, 128, 512), BF16, "scratch"),
    "gs":     ((128, NI1), F32, "in"),
    "us":     ((128, NI1), F32, "in"),
    "dsrow":  ((1, DM), F32, "in"),
    "identb": ((128, 128), BF16, "const"),
    "identf": ((128, 128), F32, "const"),
    "triu":   ((128, 128), BF16, "const"),
    "tris":   ((16, 16), F32, "const"),
    "iota":   ((128, SLACK), F32, "const"),
    "bemask": ((128, 256), F32, "const"),
    "negl":   ((128, 16), F32, "const"),
    "qfm":    ((DM, T), BF16, "scratch"),
    "kfm":    ((DM, T), BF16, "scratch"),
    "vtm":    ((T, DM), BF16, "scratch"),
    "norms":  ((128, 256), F32, "scratch"),
    "ctx":    ((T, DM), F32, "scratch"),
    "x1":     ((T, DM), F32, "scratch"),
    "pre2":   ((T, DM), F32, "scratch"),
    "x2":     ((T, DM), F32, "scratch"),
    "x3":     ((T, DM), F32, "scratch"),
    "wgb":    ((NI0, 128, DM), BF16, "scratch"),
    "wub":    ((NI0, 128, DM), BF16, "scratch"),
    "wdb":    ((NI0, 128, DM), BF16, "scratch"),
    "pre3":   ((T, DM), F32, "scratch"),
    "pre4":   ((T, DM), F32, "scratch"),
    "sc256a": ((1, 256), F32, "scratch"),
    "sc256b": ((16, 16), F32, "scratch"),
    "y":      ((T, DM), F32, "out"),
}

PHASE_IO = {
    "p01": (["x", "wq", "wk", "wv", "bq", "bk", "bv", "identb", "identf"],
            ["qfm", "kfm", "vtm", "norms"]),
    "p23": (["qfm", "kfm", "vtm", "norms", "identb", "identf", "triu", "tris",
             "iota", "bemask"], ["ctx", "sc256a", "sc256b"]),
    "ln1": (["x", "ctx"], ["x1"]),
    "pgw": (["lut", "gidxT", "uidxT", "didxT", "negl", "gwT", "uwT", "dwT",
             "wqkv", "wout"],
            ["wgb", "wub", "wdb", "gw8", "uw8", "dw8", "wqkv8", "wout8"]),
    "p5":  (["wgb", "wub", "wdb", "x1", "identb"], ["pre2"]),
    "ln2": (["x1", "pre2"], ["x2"]),
    "p7":  (["x2", "wqkv", "wout", "decay", "identb"], ["pre3"]),
    "ln3": (["x2", "pre3"], ["x3"]),
    "p8":  (["x3", "gwT", "uwT", "dwT", "gs", "us", "dsrow", "identb",
             "gw8", "uw8", "dw8"], ["pre4"]),
    "ln4": (["x3", "pre4"], ["y"]),
}
LN_ARGS = {"ln1": ("x", "ctx", "x1"), "ln2": ("x1", "pre2", "x2"),
           "ln3": ("x2", "pre3", "x3"), "ln4": ("x3", "pre4", "y")}
FM_SRC = {"p5": "x1", "p7": "x2", "p8": "x3"}
LAUNCH_GROUPS = [["p01"], ["p23"], ["ln1"], ["pgw"], ["p5"], ["ln2"],
                 ["p7"], ["ln3"], ["p8"], ["ln4"]]


def host_prepare(inputs):
    f = np.ascontiguousarray
    d = {}
    def tile_w(w, not_, w128=128):
        return f(w.reshape(NCH, 128, not_, w128).transpose(2, 1, 0, 3)
                 .reshape(not_, 128, NCH * w128))
    d["wq"] = tile_w(inputs["L0_attn_Wq"], NCH)
    d["wk"] = tile_w(inputs["L0_attn_Wk"], NCH)
    d["wv"] = tile_w(inputs["L0_attn_Wv"], NCH)
    d["bq"] = f(inputs["L0_attn_bq"].reshape(NCH, 128).T)
    d["bk"] = f(inputs["L0_attn_bk"].reshape(NCH, 128).T)
    d["bv"] = f(inputs["L0_attn_bv"].reshape(NCH, 128).T)
    d["lut"] = f(inputs["L0_ffn_lut"].reshape(1, 16))
    d["gidxT"] = tile_w(inputs["L0_ffn_gate_idx"].T, NI0)
    d["uidxT"] = tile_w(inputs["L0_ffn_up_idx"].T, NI0)
    d["didxT"] = f(inputs["L0_ffn_down_idx"].T)
    d["wqkv"] = f(inputs["L1_attn_w_qkv"].reshape(NCH, 128, 3, NH, 128)
                  .transpose(2, 3, 1, 0, 4).reshape(3, NH, 128, DM))
    d["wout"] = f(inputs["L1_attn_w_out"].reshape(NCH, 128, 4, 512)
                  .transpose(2, 0, 1, 3))
    d["decay"] = f(inputs["L1_attn_decay_param"].reshape(1, NH))
    d["gwT"] = tile_w(inputs["L1_ffn_gate_w"].T, NI1)
    d["uwT"] = tile_w(inputs["L1_ffn_up_w"].T, NI1)
    d["dwT"] = f(inputs["L1_ffn_down_w"].T.reshape(NI1, 128, DM))
    d["gs"] = f(inputs["L1_ffn_gate_s"].reshape(NI1, 128).T)
    d["us"] = f(inputs["L1_ffn_up_s"].reshape(NI1, 128).T)
    d["dsrow"] = f(inputs["L1_ffn_down_s"].reshape(1, DM))
    d["identb"] = np.eye(128, dtype=ml_dtypes.bfloat16)
    d["identf"] = np.eye(128, dtype=np.float32)
    d["triu"] = np.triu(np.ones((128, 128), np.float32)).astype(ml_dtypes.bfloat16)
    d["tris"] = np.triu(np.ones((16, 16), np.float32), 1)
    d["iota"] = np.broadcast_to(np.arange(SLACK, dtype=np.float32), (128, SLACK)).copy()
    be = np.zeros((128, 256), np.float32)
    be[0, 0:16] = 1.0
    be[127, 240:256] = 1.0
    d["bemask"] = be
    d["negl"] = np.broadcast_to(-np.arange(16, dtype=np.float32), (128, 16)).copy()
    xs = [np.ascontiguousarray(np.asarray(inputs["L0_attn_hidden_states"])[b])
          for b in range(8)]
    return d, xs


class Pgw:

    def __init__(self, k):
        self.k = k
        self.vmark = {}
        self.sttmark = {}
        self.accmark = {}
        h = k.h
        self.jobs = []
        for src_n, dst_n, tiled in (("gidxT", "wgb", True), ("uidxT", "wub", True),
                                    ("didxT", "wdb", False)):
            for c in range(NI0):
                src_ap = h[src_n][c] if tiled else h[src_n][c * 128:(c + 1) * 128, :]
                self.jobs.append((src_ap, h[dst_n][c]))
        self.i = 0
        self.half = 0
        self.done_prolog = False

    def emit_cv8(self):
        k = self.k
        P, h = k.P, k.h
        for srcn, dstn in (("gwT", "gw8"), ("uwT", "uw8"), ("dwT", "dw8")):
            for q in range(4):
                self.cv8 = P.dma("gpsimd", h[dstn][q * 11:(q + 1) * 11],
                                 h[srcn][q * 11:(q + 1) * 11], stream="cv8")
        for slot in range(3):
            self.cv8 = P.dma("gpsimd", h["wqkv8"][slot], h["wqkv"][slot],
                             stream="cv8")
        for fo in range(4):
            self.cv8 = P.dma("gpsimd", h["wout8"][fo], h["wout"][fo],
                             stream="cv8")

    def prolog(self):
        k = self.k
        P, s = k.P, k.s
        self.cv8 = None
        P.op("vector", "memset", s["PGones"][:], 1.0)
        P.op("vector", "memset", s["PG1"][:], 1.0, inc="vg")
        cl = P.dma("sync", s["LUT1"][:], k.h["lut"], stream="cg")
        cn = P.dma("sync", s["NEGL"][:], k.h["negl"], stream="cg")
        P.wait("tensor", "dS_cg", cn)
        P.wait("scalar", "dS_cg", cn)
        P.wait("tensor", "vg", P.sems["vg"].n)
        P.op("tensor", "matmul", k.ps[5][0:128, 0:16], s["PGones"][:], s["LUT1"][:],
             start=True, stop=True, inc="tg")
        P.wait("vector", "tg", P.sems["tg"].n)
        P.op("vector", "tensor_copy", s["LUTB"][:], k.ps[5][:, 0:16], inc="vg")
        self.done_prolog = True

    def step(self, n=1):
        k = self.k
        P, s = k.P, k.s
        IDX5, IDXB, TMP5, WACC, LUTB = (s["IDX5"], s["IDXB"], s["TMP5"],
                                        s["WACC"], s["LUTB"])
        for _ in range(n):
            if self.i >= len(self.jobs):
                return
            nblk = self.i
            hf = nblk % 2
            wslc = WACC[:, hf * 2048:(hf + 1) * 2048]
            islc = IDXB[:, hf * 2048:(hf + 1) * 2048]
            if self.half == 0:
                src_ap, dst_ap = self.jobs[nblk]
                h4 = nblk % 2
                if nblk >= 2:
                    P.wait("sync", "vg", self.vmark[nblk - 2])
                wi = P.dma("sync", IDX5[:, h4 * 2048:(h4 + 1) * 2048], src_ap,
                           stream="ixg" + str(h4))
                P.wait("vector", "dS_ixg" + str(h4), wi)
                P.wait("vector", "dG_wog" + str(hf))
                self.vmark[nblk] = P.op("vector", "tensor_copy", islc,
                                        IDX5[:, h4 * 2048:(h4 + 1) * 2048],
                                        inc="vg")
                for l in range(8):
                    P.wait("vector", "vg", P.sems["vg"].n)
                    if l == 0:
                        P.op("vector", "tensor_scalar", wslc, islc, 0.0,
                             LUTB[:, 0:1], ALU.is_equal, ALU.mult, inc="vg")
                    else:
                        P.op("vector", "tensor_scalar", TMP5[:, 0:2048], islc,
                             float(l), LUTB[:, l:l + 1], ALU.is_equal, ALU.mult,
                             inc="vg")
                        P.wait("vector", "vg", P.sems["vg"].n)
                        P.op("vector", "tensor_add", wslc, wslc,
                             TMP5[:, 0:2048], inc="vg")
                self.half = 1
            else:
                _, dst_ap = self.jobs[nblk]
                for l in range(8, 16):
                    P.wait("vector", "vg", P.sems["vg"].n)
                    P.op("vector", "tensor_scalar", TMP5[:, 0:2048], islc,
                         float(l), LUTB[:, l:l + 1], ALU.is_equal, ALU.mult,
                         inc="vg")
                    P.wait("vector", "vg", P.sems["vg"].n)
                    P.op("vector", "tensor_add", wslc, wslc, TMP5[:, 0:2048],
                         inc="vg")
                P.wait("gpsimd", "vg", P.sems["vg"].n)
                P.dma("gpsimd", dst_ap, wslc, stream="wog" + str(hf))
                self.half = 0
                self.i += 1

    def flush(self):
        self.step(2 * (len(self.jobs) - self.i) + self.half)



class K:
    def __init__(self, nc, P, h, s, ps):
        self.nc, self.P, self.h, self.s, self.ps = nc, P, h, s, ps
        self.pgw = None
        self.fused = False

    def pgw_step(self, n=1):
        if self.pgw is not None:
            self.pgw.step(n)


def dma_wtile(P, queue, dst_ap, W, c, fb, fbw=128):
    return P.dma(queue, dst_ap, W[c * 128:(c + 1) * 128, fb * fbw:(fb + 1) * fbw])


def emit_p01(k: K):
    P, h, s, ps = k.P, k.h, k.s, k.ps
    XFM, XF32 = s["XFM"], s["XF32"]
    XT, XBF = s["XT"], s["KVBF"]
    WF, WBF = s["WF"], s["WBF"]
    QT, SQ = s["QT"], s["SQ"]
    KVBF = s["KVBF"]
    QBF = KVBF[:, 0:2048]
    VF = s["QT"][:, 0:256].bitcast(BF16)
    NORMS2 = s["NORMS2"]
    idb, idf, onesf = s["identb"], s["identf"], s["onesf"]
    BQ, BK, BV = s["bq"], s["bk"], s["bv"]
    PS_MM, PS_TR, PS_NRM = ps[0], ps[1], ps[2]
    PS_B = ps[6]

    P.op("vector", "memset", onesf[:], 1.0, inc="v")
    for cn in ("identb", "identf", "bq", "bk", "bv"):
        P.dma("sync", s[cn][:], h[cn], stream="c")
    P.wait("tensor", "dS_c")
    P.wait("scalar", "dS_c")
    xr = h["x"]

    for j in range(NT):
        P.wait("sync", "v")
        P.dma("sync", XT[:], xr[j * 128:(j + 1) * 128, :], stream="x")
        P.wait("vector", "dS_x")
        P.wait("vector", "t")
        P.op("vector", "tensor_copy", XBF[:], XT[:], inc="v")
        P.wait("tensor", "v")
        for half in range(2):
            P.wait("tensor", "v")
            for cc in range(8):
                c = half * 8 + cc
                P.op("tensor", "transpose", PS_B[:, cc * 128:(cc + 1) * 128],
                     XBF[:, c * 128:(c + 1) * 128], idb[:],
                     inc="t" if cc == 7 else None)
            P.wait("vector", "t")
            for cc in range(8):
                c = half * 8 + cc
                P.op("vector", "tensor_copy",
                     XFM[:, c * 2048 + j * 128: c * 2048 + (j + 1) * 128],
                     PS_B[:, cc * 128:(cc + 1) * 128],
                     inc="v" if cc == 7 else None)
    P.barrier()

    for tb in range(4):
        for sub in range(4):
            j = tb * 4 + sub
            P.wait("sync", "t")
            P.dma("sync", XT[:], xr[j * 128:(j + 1) * 128, :], stream="x")
            P.wait("tensor", "dS_x")
            for half in range(4):
                P.wait("tensor", "v")
                for cc in range(4):
                    c = half * 4 + cc
                    P.op("tensor", "matmul", PS_TR[:, cc * 128:(cc + 1) * 128],
                         XT[:, c * 128:(c + 1) * 128], idf[:],
                         start=True, stop=True,
                         inc="t" if cc == 3 else None)
                P.wait("vector", "t")
                for cc in range(4):
                    c = half * 4 + cc
                    P.op("vector", "tensor_copy",
                         XF32[:, c * 512 + sub * 128: c * 512 + sub * 128 + 128],
                         PS_TR[:, cc * 128:(cc + 1) * 128],
                         inc="v" if cc == 3 else None)
        P.wait("tensor", "v")
        tmark = {}
        a_ev, v_sq = {}, {}
        for fb in range(NCH):
            hf = fb % 2
            h4 = fb % 4
            PSq = ps[0] if hf == 0 else ps[4]
            QTp = QT[:, h4 * 512:(h4 + 1) * 512]
            SQp = SQ[:, h4 * 512:(h4 + 1) * 512]
            QBp = QBF[:, h4 * 512:(h4 + 1) * 512]
            if fb >= 2:
                P.wait("sync", "t", tmark[fb - 2])
            wv_ = P.dma("sync", WF[:, hf * 2048:(hf + 1) * 2048],
                        h["wq"][fb], stream="wq" + str(hf))
            P.wait("tensor", "dS_wq" + str(hf), wv_)
            if fb - 2 in a_ev:
                P.wait("tensor", "a", a_ev[fb - 2])
            else:
                P.wait("tensor", "a")
            for c in range(NCH):
                tmark[fb] = P.op("tensor", "matmul", PSq[:],
                     WF[:, hf * 2048 + c * 128: hf * 2048 + (c + 1) * 128],
                     XF32[:, c * 512:(c + 1) * 512],
                     start=(c == 0), stop=(c == NCH - 1),
                     inc="t" if c == NCH - 1 else None)
            P.wait("scalar", "t", tmark[fb])
            if fb - 4 in v_sq:
                P.wait("scalar", "v", v_sq[fb - 4])
            P.op("scalar", "activation", QTp, PSq[:], AF.Identity,
                 bias=BQ[:, fb:fb + 1], scale=1.0, inc="a")
            a_ev[fb] = P.sems["a"].n
            P.wait("vector", "a", a_ev[fb])
            P.wait("vector", "dG_qo" + str(h4))
            P.op("vector", "tensor_mul", SQp, QTp, QTp)
            v_sq[fb] = P.op("vector", "tensor_copy", QBp, QTp, inc="v")
            P.wait("tensor", "v", v_sq[fb])
            for sub in range(4):
                j = tb * 4 + sub
                col = 2 * (j * 16 + fb)
                P.op("tensor", "matmul", PS_NRM[:, col:col + 1],
                     SQp[:, sub * 128:(sub + 1) * 128], onesf[:],
                     start=True, stop=True, inc="t" if sub == 3 else None)
            P.wait("gpsimd", "v", v_sq[fb])
            P.dma("gpsimd", h["qfm"][fb * 128:(fb + 1) * 128, tb * 512:(tb + 1) * 512],
                  QBp, stream="qo" + str(h4))
            k.pgw_step(1)
    P.wait("vector", "t")
    P.op("vector", "tensor_copy", NORMS2[:],
         PS_NRM[:].rearrange("p (c two) -> p c two", two=2)[:, :, 0:1].squeeze(2),
         inc="v")
    P.wait("gpsimd", "v")
    P.dma("gpsimd", h["norms"], NORMS2[:], stream="no")
    P.barrier()

    for which in ("k", "v"):
        W = h["wk"] if which == "k" else h["wv"]
        Bp = BK if which == "k" else BV
        tmark = {}
        kv_aev = {}
        for fb in range(NCH):
            hf = fb % 2
            if fb >= 2:
                P.wait("gpsimd", "t", tmark[fb - 2])
            wv_ = P.dma("gpsimd", WBF[:, hf * 2048:(hf + 1) * 2048],
                        W[fb], stream="w" + which + str(hf))
            P.wait("tensor", "dG_w" + which + str(hf), wv_)
            for tb in range(4):
                kpar = (fb * 4 + tb) % 2
                PSkv = ps[0] if kpar == 0 else ps[3]
                if fb * 4 + tb >= 2 and (fb * 4 + tb - 2) in kv_aev:
                    P.wait("tensor", "a", kv_aev[fb * 4 + tb - 2])
                else:
                    P.wait("tensor", "a")
                for c in range(NCH):
                    tmark[fb] = P.op("tensor", "matmul", PSkv[:],
                         WBF[:, hf * 2048 + c * 128: hf * 2048 + (c + 1) * 128],
                         XFM[:, c * 2048 + tb * 512: c * 2048 + (tb + 1) * 512],
                         start=(c == 0), stop=(c == NCH - 1),
                         inc="t" if c == NCH - 1 else None)
                P.wait("scalar", "t", tmark[fb])
                if which == "k":
                    P.wait("scalar", "dG_kv")
                    P.op("scalar", "activation", KVBF[:, tb * 512:(tb + 1) * 512],
                         PSkv[:], AF.Identity, bias=Bp[:, fb:fb + 1], scale=1.0,
                         inc="a")
                    kv_aev[fb * 4 + tb] = P.sems["a"].n
                else:
                    P.wait("scalar", "t")
                    P.op("scalar", "activation", VF, PSkv[:], AF.Identity,
                         bias=Bp[:, fb:fb + 1], scale=1.0, inc="a")
                    kv_aev[fb * 4 + tb] = P.sems["a"].n
                    P.wait("tensor", "a")
                    P.wait("tensor", "v")
                    for sub in range(4):
                        P.op("tensor", "transpose", PS_B[:, sub * 128:(sub + 1) * 128],
                             VF[:, sub * 128:(sub + 1) * 128], idb[:],
                             inc="t" if sub == 3 else None)
                    P.wait("vector", "t")
                    P.wait("vector", "dG_kv")
                    for sub in range(4):
                        P.op("vector", "tensor_copy",
                             KVBF[:, sub * 128:(sub + 1) * 128],
                             PS_B[:, sub * 128:(sub + 1) * 128],
                             inc="v" if sub == 3 else None)
                    P.wait("gpsimd", "v")
                    P.dma("gpsimd",
                          h["vtm"].rearrange("(jj p) f -> p jj f", p=128)[
                              :, tb * 4:(tb + 1) * 4, fb * 128:(fb + 1) * 128],
                          KVBF[:, 0:512].rearrange("p (s d) -> p s d", s=4),
                          stream="kv")
            if which == "k":
                P.wait("gpsimd", "a")
                P.dma("gpsimd", h["kfm"][fb * 128:(fb + 1) * 128, :], KVBF[:],
                      stream="kv")
                k.pgw_step(2)
            else:
                k.pgw_step(2)
    P.barrier()




def emit_p23(k: K):
    P, h, s, ps = k.P, k.h, k.s, k.ps
    if k.fused and k.pgw is not None:
        k.pgw.emit_cv8()
    NORMS2 = s["NORMS2"]
    MASK, MASKB, INVM, RANK = s["MASK"], s["MASKB"], s["INVM"], s["RANK"]
    CMP = s["CMP"]
    LOa, LOb, HIa, HIb = s["LOa"], s["LOb"], s["HIa"], s["HIb"]
    MID, MIDX, CNT16, GE, TT16 = s["MID"], s["MIDX"], s["CNT16"], s["GE"], s["TT16"]
    M1, HROW = s["M1"], s["HROW"]
    TOTS, TOTJ, OFFJS, OFFROW, PINCS = s["TOTS"], s["TOTJ"], s["OFFJS"], s["OFFROW"], s["PINCS"]
    idb, idf = s["identb"], s["identf"]
    onesf, onesb, onesrf = s["onesf"], s["onesb"], s["onesrf"]
    TRIU, TRIS, IOTA, BE = s["triu"], s["tris"], s["iota"], s["bemask"]
    QH, KH, VH, QTM = s["QH"], s["KH"], s["VH"], s["QTM"]
    S2, S2T = s["S2"], s["S2T"]
    PR, PT, CTXH, CTXO = s["PR"], s["PT"], s["CTXH"], s["CTXO"]
    MX, NM, RS = s["MX"], s["NM"], s["RS"]
    QGB, GSF, GP, GPT, GCF = s["QGB"], s["GSF"], s["GP"], s["GPT"], s["GCF"]
    MXG, NMG, RSG = s["MXG"], s["NMG"], s["RSG"]
    PS_SC, PS_CT, PS_G, PS_X = ps[0], ps[1], ps[2], ps[3]
    PS_CT2 = ps[3]
    PS_B0, PS_B1 = ps[6], ps[7]

    P.op("vector", "memset", onesf[:], 1.0)
    P.op("vector", "memset", onesb[:], 1.0)
    P.op("vector", "memset", onesrf[:], 1.0)
    P.op("vector", "memset", LOa[:], -1.0, inc="v")
    for cn in ("identb", "identf", "triu", "tris", "iota", "bemask"):
        P.dma("sync", s[cn][:], h[cn], stream="c")
    cn_ = P.dma("sync", NORMS2[:], h["norms"], stream="c")
    P.wait("tensor", "dS_c")
    P.wait("vector", "dS_c")
    P.wait("scalar", "dS_c")

    P.op("vector", "tensor_reduce", M1[:],
         NORMS2[:].rearrange("p (j h) -> p h j", h=16),
         AX.X, ALU.max, inc="v")
    P.wait("tensor", "v")
    P.op("tensor", "matmul", PS_G[0:16, 0:128], M1[:], idf[:], start=True,
         stop=True, inc="t")
    P.wait("vector", "t")
    P.op("vector", "tensor_reduce", s["H16C"][:], PS_G[0:16, 0:128], AX.X, ALU.max,
         inc="v")
    P.wait("tensor", "v")
    P.op("tensor", "matmul", PS_X[0:1, 0:16], s["H16C"][:], idf[0:16, 0:16],
         start=True, stop=True, inc="t")
    P.wait("vector", "t")
    P.op("vector", "tensor_copy", HIa[:], PS_X[0:1, 0:16], inc="v")

    for it in range(NBIS):
        LO = LOa if it % 2 == 0 else LOb
        HI = HIa if it % 2 == 0 else HIb
        LO2 = LOb if it % 2 == 0 else LOa
        HI2 = HIb if it % 2 == 0 else HIa
        P.wait("vector", "v")
        P.op("vector", "tensor_add", TT16[:], LO[:], HI[:], inc="v")
        P.wait("vector", "v")
        P.op("vector", "tensor_scalar_mul", MID[:], TT16[:], 0.5, inc="v")
        P.wait("vector", "v")
        P.op("vector", "tensor_copy",
             MIDX[:].rearrange("a (j h) -> a j h", h=16),
             MID[:].unsqueeze(1).broadcast_to([1, 16, 16]), inc="v")
        P.wait("tensor", "v")
        P.op("tensor", "matmul", PS_G[:, 0:256], onesrf[:], MIDX[:],
             start=True, stop=True, inc="t")
        P.wait("vector", "t")
        P.op("vector", "tensor_tensor", CMP[:], NORMS2[:], PS_G[:, 0:256],
             ALU.is_gt, inc="v")
        P.wait("tensor", "v")
        P.op("tensor", "matmul", PS_X[0:1, 0:256], onesb[:], CMP[:],
             start=True, stop=True, inc="t")
        P.wait("vector", "t")
        P.op("vector", "tensor_reduce", CNT16[:],
             PS_X[0:1, 0:256].rearrange("a (j h) -> a h j", h=16),
             AX.X, ALU.add, inc="v")
        P.wait("vector", "v")
        P.op("vector", "tensor_single_scalar", GE[:], CNT16[:], 62.0, ALU.is_ge,
             inc="v")
        P.wait("vector", "v")
        P.op("vector", "select", LO2[:], GE[:], MID[:], LO[:], True, inc="v")
        P.wait("vector", "v")
        P.op("vector", "select", HI2[:], GE[:], HI[:], MID[:], True, inc="v")
        P.wait("vector", "v")
    LOF = LOa if NBIS % 2 == 0 else LOb

    P.op("vector", "tensor_copy",
         MIDX[:].rearrange("a (j h) -> a j h", h=16),
         LOF[:].unsqueeze(1).broadcast_to([1, 16, 16]), inc="v")
    P.wait("tensor", "v")
    P.op("tensor", "matmul", PS_G[:, 0:256], onesrf[:], MIDX[:],
         start=True, stop=True, inc="t")
    P.wait("vector", "t")
    P.op("vector", "tensor_tensor", MASK[:], NORMS2[:], PS_G[:, 0:256], ALU.is_gt,
         inc="v")
    P.wait("vector", "v")
    P.op("vector", "tensor_max", MASK[:], MASK[:], BE[:], inc="v")
    P.wait("vector", "v")
    P.op("vector", "tensor_copy", MASKB[:], MASK[:])
    P.op("vector", "tensor_scalar", INVM[:], MASK[:], -1.0, 1.0, ALU.mult, ALU.add,
         inc="v")
    P.wait("tensor", "v")
    P.op("tensor", "matmul", PS_CT[:, 0:256], TRIU[:], MASKB[:],
         start=True, stop=True)
    P.op("tensor", "matmul", PS_X[0:1, 0:256], onesb[:], MASKB[:],
         start=True, stop=True, inc="t")
    P.wait("vector", "t")
    P.op("vector", "tensor_copy", TOTS[:], PS_X[0:1, 0:256])
    P.op("vector", "tensor_copy", PINCS[:], PS_CT[:, 0:256], inc="v")
    P.wait("gpsimd", "v")
    P.dma("gpsimd", h["sc256a"], TOTS[:], stream="r")
    P.wait("gpsimd", "dG_r")
    P.dma("gpsimd", TOTJ[:], h["sc256a"].rearrange("a (j h) -> j (a h)", h=16),
          stream="r")
    P.wait("tensor", "dG_r")
    P.op("tensor", "matmul", PS_X[0:16, 256:272], TRIS[:], TOTJ[:],
         start=True, stop=True, inc="t")
    P.wait("vector", "t")
    P.op("vector", "tensor_copy", OFFJS[:], PS_X[0:16, 256:272], inc="v")
    P.wait("gpsimd", "v")
    P.dma("gpsimd", h["sc256b"], OFFJS[:], stream="r")
    P.wait("gpsimd", "dG_r")
    P.dma("gpsimd", OFFROW[:], h["sc256b"].rearrange("j h -> (j h)").unsqueeze(0), stream="r")
    P.wait("tensor", "dG_r")
    P.op("tensor", "matmul", PS_G[:, 0:256], onesrf[:], OFFROW[:],
         start=True, stop=True, inc="t")
    P.wait("vector", "t")
    P.op("vector", "scalar_tensor_tensor", RANK[:], PINCS[:], -1.0, PS_G[:, 0:256],
         ALU.add, ALU.add, inc="v")
    P.barrier()

    s1m = {}
    s2m = {}

    def stage1(fb):
        par = fb % 2
        QHh = QH[:, par * 2048:(par + 1) * 2048]
        KHh = KH[:, par * 2048:(par + 1) * 2048]
        VHh = VH[:, par * 2048:(par + 1) * 2048]
        QTMh = QTM[:, par * 2048:(par + 1) * 2048]
        S2h = S2[:, par * NT * SLACK:(par + 1) * NT * SLACK]
        S2Th = S2T[:, par * 2048:(par + 1) * 2048]
        CTXHh = CTXH[:, par * 2048:(par + 1) * 2048]
        if fb - 2 in s2m:
            P.wait("sync", "t", s2m[fb - 2]["t"])
            P.wait("sync", "v", s2m[fb - 2]["v"])
            P.wait("gpsimd", "t", s2m[fb - 2]["t"])
        else:
            P.wait("sync", "t")
            P.wait("gpsimd", "t")
        wq_ = P.dma("sync", QHh, h["qfm"][fb * 128:(fb + 1) * 128, :],
                    stream="qh" + str(par))
        wk_ = P.dma("sync", KHh, h["kfm"][fb * 128:(fb + 1) * 128, :],
                    stream="qh" + str(par))
        wv_ = P.dma("gpsimd", VHh.rearrange("p (j d) -> p j d", d=128),
                    h["vtm"].rearrange("(j p) f -> p j f", p=128)[
                        :, :, fb * 128:(fb + 1) * 128], stream="vh" + str(par))
        P.wait("tensor", "dS_qh" + str(par), wk_)
        for half in range(2):
            P.wait("tensor", "a")
            for cc in range(8):
                j = half * 8 + cc
                P.op("tensor", "transpose", PS_B0[:, cc * 128:(cc + 1) * 128],
                     QHh[:, j * 128:(j + 1) * 128], idb[:],
                     inc="t" if cc == 7 else None)
            P.wait("scalar", "t")
            for cc in range(8):
                j = half * 8 + cc
                P.op("scalar", "copy", QTMh[:, j * 128:(j + 1) * 128],
                     PS_B0[:, cc * 128:(cc + 1) * 128],
                     inc="a" if cc == 7 else None)
        P.wait("vector", "t")
        P.wait("vector", "v")
        for j in range(NT):
            col = j * 16 + fb
            P.op("vector", "tensor_scalar", S2h[:, j * SLACK:(j + 1) * SLACK],
                 IOTA[:], RANK[:, col:col + 1], MASK[:, col:col + 1],
                 ALU.is_equal, ALU.mult, inc="v" if j == NT - 1 else None)
        P.wait("tensor", "v")
        for half in range(2):
            P.wait("tensor", "a")
            for cc in range(8):
                j = half * 8 + cc
                P.op("tensor", "transpose", PS_B1[0:72, cc * 128:cc * 128 + 128],
                     S2h[:, j * SLACK:(j + 1) * SLACK], idb[:],
                     inc="t" if cc == 7 else None)
            P.wait("scalar", "t")
            for cc in range(8):
                j = half * 8 + cc
                P.op("scalar", "copy", S2Th[0:72, j * 128:(j + 1) * 128],
                     PS_B1[0:72, cc * 128:cc * 128 + 128],
                     inc="a" if cc == 7 else None)
        P.wait("tensor", "dS_qh" + str(par), wk_)
        P.wait("tensor", "dG_vh" + str(par), wv_)
        P.wait("vector", "t")
        P.op("vector", "memset", PR[:], 0.0, inc="v")
        t_sc, t_tr, t_pv = {}, {}, {}
        a_exp, a_ctx = {}, {}
        v_max, v_pt, v_rs = {}, {}, {}
        for qb in range(NT):
            qpar = qb % 2
            SB = ps[0] if qpar == 0 else ps[4]
            CTb = ps[1] if qpar == 0 else ps[5]
            PRh = PR[:, qpar * 384:(qpar + 1) * 384]
            PTh = PT[:, qpar * 384:(qpar + 1) * 384]
            cl0 = max(qb - 1, 0)
            cl1 = min(qb + 2, NT)
            W = (cl1 - cl0) * 128
            v0 = max(qb * 128 - 64 - cl0 * 128, 0)
            v1 = min(qb * 128 + 192 - cl0 * 128, W)
            nch = cl1 - cl0
            if qb - 2 in a_exp:
                P.wait("tensor", "a", a_exp[qb - 2])
                P.wait("tensor", "v", v_max[qb - 2])
            else:
                P.wait("tensor", "v")
            for i, ch in enumerate(range(cl0, cl1)):
                t_sc[qb] = P.op("tensor", "matmul", SB[:, i * 128:(i + 1) * 128],
                     QHh[:, qb * 128:(qb + 1) * 128],
                     KHh[:, ch * 128:(ch + 1) * 128],
                     start=True, stop=True, inc="t" if ch == cl1 - 1 else None)
            P.wait("vector", "t", t_sc[qb])
            v_max[qb] = P.op("vector", "reduce_max", MX[:, qpar:qpar + 1],
                             SB[:, v0:v1], axis=AX.X, inc="v")
            P.wait("vector", "v", v_max[qb])
            P.op("vector", "tensor_scalar_mul", NM[:, qpar:qpar + 1],
                 MX[:, qpar:qpar + 1], -SCALE, inc="v")
            if qb == 2:
                P.wait("vector", "t", t_tr[0])
                P.op("vector", "memset", PR[:, 0:64], 0.0, inc="v")
            P.wait("scalar", "v")
            P.op("scalar", "activation", PRh[:, v0:v1], SB[:, v0:v1], AF.Exp,
                 bias=NM[:, qpar:qpar + 1], scale=SCALE,
                 accum_out=s["SUMS"][:, qpar:qpar + 1], inc="a")
            a_exp[qb] = P.sems["a"].n
            P.wait("vector", "a", a_exp[qb])
            v_rs[qb] = P.op("vector", "reciprocal", RS[:, qpar:qpar + 1],
                            s["SUMS"][:, qpar:qpar + 1], inc="v")
            P.wait("tensor", "a", a_exp[qb])
            if qb - 2 in v_pt:
                P.wait("tensor", "a", v_pt[qb - 2])
            for i in range(nch):
                t_tr[qb] = P.op("tensor", "transpose",
                     PS_B0[:, qpar * 384 + i * 128: qpar * 384 + (i + 1) * 128],
                     PRh[:, i * 128:(i + 1) * 128], idb[:],
                     inc="t" if i == nch - 1 else None)
            P.wait("scalar", "t", t_tr[qb])
            if qb - 2 in t_pv:
                P.wait("scalar", "t", t_pv[qb - 2])
            for i in range(nch):
                P.op("scalar", "copy",
                     PTh[:, i * 128:(i + 1) * 128],
                     PS_B0[:, qpar * 384 + i * 128: qpar * 384 + (i + 1) * 128],
                     inc="a" if i == nch - 1 else None)
            v_pt[qb] = P.sems["a"].n
            P.wait("tensor", "a", v_pt[qb])
            if qb - 2 in a_ctx:
                P.wait("tensor", "a", a_ctx[qb - 2])
            for i, ch in enumerate(range(cl0, cl1)):
                t_pv[qb] = P.op("tensor", "matmul", CTb[:, 0:128],
                     PTh[:, i * 128:(i + 1) * 128],
                     VHh[:, ch * 128:(ch + 1) * 128],
                     start=(i == 0), stop=(i == nch - 1),
                     inc="t" if i == nch - 1 else None)
            P.wait("scalar", "t", t_pv[qb])
            P.wait("scalar", "v", v_rs[qb])
            P.op("scalar", "activation", CTXHh[:, qb * 128:(qb + 1) * 128],
                 CTb[:, 0:128], AF.Identity, scale=RS[:, qpar:qpar + 1],
                 inc="a")
            a_ctx[qb] = P.sems["a"].n
            if qb in (4, 9, 14):
                k.pgw_step(1)
        s1m[fb] = {"t": P.sems["t"].n, "v": P.sems["v"].n, "a": P.sems["a"].n}

    def stage2(fb):
        par = fb % 2
        QHh = QH[:, par * 2048:(par + 1) * 2048]
        KHh = KH[:, par * 2048:(par + 1) * 2048]
        VHh = VH[:, par * 2048:(par + 1) * 2048]
        QTMh = QTM[:, par * 2048:(par + 1) * 2048]
        S2h = S2[:, par * NT * SLACK:(par + 1) * NT * SLACK]
        S2Th = S2T[:, par * 2048:(par + 1) * 2048]
        CTXHh = CTXH[:, par * 2048:(par + 1) * 2048]
        CTXOh = CTXO[:, par * 2048:(par + 1) * 2048]
        P.wait("tensor", "v", s1m[fb]["v"])
        P.wait("tensor", "a", s1m[fb]["a"])
        for j in range(NT):
            P.op("tensor", "matmul", PS_G[:, 0:SLACK],
                 QTMh[:, j * 128:(j + 1) * 128],
                 S2h[:, j * SLACK:(j + 1) * SLACK],
                 start=(j == 0), stop=(j == NT - 1),
                 inc="t" if j == NT - 1 else None)
        P.wait("vector", "t")
        P.op("vector", "tensor_copy", QGB[:], PS_G[:, 0:SLACK], inc="v")
        P.wait("tensor", "v")
        for tb in range(4):
            P.op("tensor", "matmul", PS_X[0:SLACK, 0:512], QGB[:],
                 KHh[:, tb * 512:(tb + 1) * 512], start=True, stop=True,
                 inc="t")
            P.wait("scalar", "t")
            P.op("scalar", "copy", GSF[:, tb * 512:(tb + 1) * 512],
                 PS_X[0:SLACK, 0:512], inc="a")
            P.wait("tensor", "a")
        P.wait("vector", "a")
        P.op("vector", "reduce_max", MXG[:], GSF[:], axis=AX.X, inc="v")
        P.wait("vector", "v")
        P.op("vector", "tensor_scalar_mul", NMG[:], MXG[:], -SCALE, inc="v")
        P.wait("scalar", "v")
        P.op("scalar", "activation", GP[:], GSF[:], AF.Exp, bias=NMG[:, 0:1],
             scale=SCALE, accum_out=s["GSUM"][:], inc="a")
        P.wait("vector", "a")
        P.op("vector", "reciprocal", RSG[:], s["GSUM"][:], inc="v")
        k.pgw_step(1)
        P.wait("tensor", "a")
        for half in range(2):
            P.wait("tensor", "a")
            for cc in range(8):
                j = half * 8 + cc
                P.op("tensor", "transpose", PS_B1[:, cc * 128:cc * 128 + 72],
                     GP[0:72, j * 128:(j + 1) * 128], idb[0:72, 0:72],
                     inc="t" if cc == 7 else None)
            P.wait("scalar", "t")
            for cc in range(8):
                j = half * 8 + cc
                P.op("scalar", "copy", GPT[:, j * SLACK:(j + 1) * SLACK],
                     PS_B1[:, cc * 128:cc * 128 + 72],
                     inc="a" if cc == 7 else None)
        P.wait("tensor", "v")
        P.wait("tensor", "a")
        for j in range(NT):
            P.op("tensor", "matmul", PS_G[0:SLACK, 0:128],
                 GPT[:, j * SLACK:(j + 1) * SLACK],
                 VHh[:, j * 128:(j + 1) * 128],
                 start=(j == 0), stop=(j == NT - 1),
                 inc="t" if j == NT - 1 else None)
        P.wait("vector", "t")
        P.op("vector", "tensor_scalar_mul", GCF[:], PS_G[0:SLACK, 0:128],
             RSG[:, 0:1], inc="v")
        P.wait("tensor", "v")
        P.wait("tensor", "a")
        P.wait("vector", "a")
        P.wait("vector", "dG_co" + str(par))
        for j in range(NT):
            col = j * 16 + fb
            P.op("tensor", "matmul", PS_CT2[:, 0:128],
                 S2Th[0:72, j * 128:(j + 1) * 128], GCF[:],
                 start=True, stop=True, inc="t")
            P.wait("vector", "t")
            P.op("vector", "scalar_tensor_tensor", CTXOh[:, j * 128:(j + 1) * 128],
                 CTXHh[:, j * 128:(j + 1) * 128], INVM[:, col:col + 1],
                 PS_CT2[:, 0:128], ALU.mult, ALU.add, inc="v")
            P.wait("tensor", "v")
        P.wait("gpsimd", "v")
        P.wait("gpsimd", "dG_co" + str(par))
        P.dma("gpsimd",
              h["ctx"].rearrange("(j p) f -> p j f", p=128)[:, :, fb * 128:(fb + 1) * 128],
              CTXOh.rearrange("p (j d) -> p j d", d=128), stream="co" + str(par))
        k.pgw_step(1 if fb < 13 else 4)
        s2m[fb] = {"t": P.sems["t"].n, "v": P.sems["v"].n, "a": P.sems["a"].n}

    for fb in range(NH + 1):
        if fb < NH:
            stage1(fb)
        if fb >= 1:
            stage2(fb - 1)
    if k.pgw is not None:
        k.pgw.flush()
    P.barrier()



def emit_ln(k: K, a_name, b_name, out_name, build_fm=False):
    P, h, s = k.P, k.h, k.s
    XT4, CT4, SQ4, X1T = s["XT4"], s["CT4"], s["SQ4"], s["X1T"]
    SM, VARS, RSTD = s["SM"], s["VARS"], s["RSTD"]
    if build_fm:
        RFM, idb, PS_B = s["RFM"], s["identb"], k.ps[6]
        P.dma("sync", idb[:], h["identb"], stream="c")
        P.wait("tensor", "dS_c")
    P.op("vector", "memset", s["EPS4"][:], 1e-5, inc="v")
    P.wait("scalar", "v")
    mk = {}
    for j in range(NT):
        par = j % 2
        XTp = XT4[:, par * 2048:(par + 1) * 2048]
        CTp = CT4[:, par * 2048:(par + 1) * 2048]
        X1p = X1T[:, par * 2048:(par + 1) * 2048]
        SQp = SQ4[:, par * 2048:(par + 1) * 2048]
        SMc = SM[:, par:par + 1]
        VRc = VARS[:, par:par + 1]
        RSc = RSTD[:, par:par + 1]
        if j >= 2:
            P.wait("sync", "v", mk[("x1", j - 2)])
            P.wait("sync", "a", mk[("sq", j - 2)])
        wa = P.dma("sync", XTp, h[a_name][j * 128:(j + 1) * 128, :],
                   stream="la" + str(par))
        wb = P.dma("sync", CTp, h[b_name][j * 128:(j + 1) * 128, :],
                   stream="lb" + str(par))
        P.wait("vector", "dS_la" + str(par), wa)
        P.wait("vector", "dS_lb" + str(par), wb)
        if j >= 2 and build_fm:
            P.wait("vector", "a", mk[("sq", j - 2)])
        P.op("vector", "tensor_add", XTp, XTp, CTp, inc="v")
        P.wait("vector", "v")
        P.op("vector", "reduce_sum", SMc, XTp, axis=AX.X, inc="v")
        P.wait("vector", "v")
        P.op("vector", "tensor_scalar_mul", SMc, SMc, 1.0 / DM, inc="v")
        P.wait("vector", "v")
        mk[("ctr", j)] = P.op("vector", "tensor_scalar", XTp, XTp, SMc, None,
                              ALU.subtract, inc="v")
        P.wait("scalar", "v", mk[("ctr", j)])
        P.wait("scalar", "a")
        if j >= 2 and build_fm:
            P.wait("scalar", "t", mk[("tr", j - 2)])
        P.op("scalar", "activation", SQp, XTp, AF.Square,
             accum_out=VRc, inc="a")
        mk[("sq", j)] = P.sems["a"].n
        P.wait("scalar", "a")
        P.op("scalar", "activation", VRc, VRc, AF.Ln,
             bias=s["EPS4"][:, 0:1], scale=1.0 / DM, inc="a")
        P.wait("vector", "a")
        P.op("vector", "tensor_scalar_mul", VRc, VRc, -0.5, inc="v")
        P.wait("scalar", "v")
        P.op("scalar", "activation", RSc, VRc, AF.Exp, inc="a")
        mk[("rstd", j)] = P.sems["a"].n
        P.wait("vector", "a", mk[("rstd", j)])
        P.wait("vector", "dG_lo" + str(par))
        mk[("x1", j)] = P.op("vector", "tensor_scalar_mul", X1p, XTp,
                             RSc, inc="v")
        P.wait("gpsimd", "v", mk[("x1", j)])
        P.dma("gpsimd", h[out_name][j * 128:(j + 1) * 128, :], X1p,
              stream="lo" + str(par))
        if out_name == "x1":
            k.pgw_step(2)
        if build_fm:
            XB = SQp.bitcast(BF16)[:, 0:2048]
            P.wait("vector", "a", mk[("sq", j)])
            P.wait("vector", "v", mk[("x1", j)])
            mk[("xb", j)] = P.op("vector", "tensor_copy", XB, X1p, inc="v")
            P.wait("tensor", "v", mk[("xb", j)])
            for half in range(2):
                P.wait("tensor", "v")
                for cc in range(8):
                    c = half * 8 + cc
                    mk[("tr", j)] = P.op("tensor", "transpose",
                         PS_B[:, cc * 128:(cc + 1) * 128],
                         XB[:, c * 128:(c + 1) * 128], idb[:],
                         inc="t" if cc == 7 else None)
                P.wait("vector", "t")
                for cc in range(8):
                    c = half * 8 + cc
                    P.op("vector", "tensor_copy",
                         RFM[:, c * 2048 + j * 128: c * 2048 + (j + 1) * 128],
                         PS_B[:, cc * 128:(cc + 1) * 128],
                         inc="v" if cc == 7 else None)
    P.barrier()


def emit_ln1(k): emit_ln(k, "x", "ctx", "x1", build_fm=k.fused)
def emit_ln2(k): emit_ln(k, "x1", "pre2", "x2", build_fm=k.fused)
def emit_ln3(k): emit_ln(k, "x2", "pre3", "x3", build_fm=k.fused)
def emit_ln4(k): emit_ln(k, "x3", "pre4", "y")


def emit_fm(k: K, src_name, xb4=None):
    P, h, s, ps = k.P, k.h, k.s, k.ps
    RFM, XT4 = s["RFM"], s["XT4"]
    XB4 = xb4 if xb4 is not None else s["XB4"]
    idb = s["identb"]
    PS_B = ps[6]
    P.dma("sync", idb[:], h["identb"], stream="c")
    P.wait("tensor", "dS_c")
    for j in range(NT):
        P.wait("sync", "v")
        P.dma("sync", XT4[:, 0:2048], h[src_name][j * 128:(j + 1) * 128, :],
              stream="fx")
        P.wait("vector", "dS_fx")
        P.wait("vector", "t")
        P.op("vector", "tensor_copy", XB4, XT4[:, 0:2048], inc="v")
        P.wait("tensor", "v")
        for half in range(2):
            P.wait("tensor", "v")
            for cc in range(8):
                c = half * 8 + cc
                P.op("tensor", "transpose", PS_B[:, cc * 128:(cc + 1) * 128],
                     XB4[:, c * 128:(c + 1) * 128], idb[:],
                     inc="t" if cc == 7 else None)
            P.wait("vector", "t")
            for cc in range(8):
                c = half * 8 + cc
                P.op("vector", "tensor_copy",
                     RFM[:, c * 2048 + j * 128: c * 2048 + (j + 1) * 128],
                     PS_B[:, cc * 128:(cc + 1) * 128],
                     inc="v" if cc == 7 else None)
    P.barrier()


def emit_pgw(k: K):
    if k.pgw is None:
        k.pgw = Pgw(k)
    if not k.pgw.done_prolog:
        k.pgw.prolog()
    if not k.fused:
        k.pgw.flush()
    k.P.barrier()


def emit_p5(k: K):
    if not k.fused:
        emit_fm(k, "x1", k.s["H5"][:, 0:2048])
    P, h, s, ps = k.P, k.h, k.s, k.ps
    RFM, H5 = s["RFM"], s["H5"]
    WGB5, WUB5, WD5 = s["WGB5"], s["WUB5"], s["WD5"]
    SG5, P5E = s["SG5"], s["P5E"]
    PS_G5, PS_U5, PS_D0, PS_D1 = ps[0], ps[1], ps[2], ps[3]
    for th in range(2):
        tmark = {}
        for ot in range(NI0):
            hf = ot % 2
            if ot >= 2:
                P.wait("sync", "t", tmark[ot - 2])
                P.wait("scalar", "t", tmark[ot - 2])
            g_ = P.dma("sync", WGB5[:, hf * 2048:(hf + 1) * 2048], h["wgb"][ot],
                       stream="wg5" + str(hf))
            u_ = P.dma("scalar", WUB5[:, hf * 2048:(hf + 1) * 2048], h["wub"][ot],
                       stream="wu5" + str(hf))
            P.wait("tensor", "dS_wg5" + str(hf), g_)
            P.wait("tensor", "dA_wu5" + str(hf), u_)
            for tb2 in range(2):
                tc = th * 1024 + tb2 * 512
                par = tb2 % 2
                PS_G5 = k.ps[0] if par == 0 else k.ps[2]
                PS_U5 = k.ps[1] if par == 0 else k.ps[3]
                if ot == 0:
                    P.wait("tensor", "a")
                    P.wait("tensor", "v")
                else:
                    P.wait("tensor", "a", max(P.sems["a"].n - 1, 0) if "a" in P.sems else 0)
                    P.wait("tensor", "v", max(P.sems["v"].n - 1, 0) if "v" in P.sems else 0)
                for c in range(NCH):
                    P.op("tensor", "matmul", PS_G5[:],
                         WGB5[:, hf * 2048 + c * 128: hf * 2048 + (c + 1) * 128],
                         RFM[:, c * 2048 + tc: c * 2048 + tc + 512],
                         start=(c == 0), stop=(c == NCH - 1))
                for c in range(NCH):
                    tmark[ot] = P.op("tensor", "matmul", PS_U5[:],
                         WUB5[:, hf * 2048 + c * 128: hf * 2048 + (c + 1) * 128],
                         RFM[:, c * 2048 + tc: c * 2048 + tc + 512],
                         start=(c == 0), stop=(c == NCH - 1),
                         inc="t" if c == NCH - 1 else None)
                P.wait("scalar", "t")
                P.wait("scalar", "v")
                P.op("scalar", "activation", SG5[:], PS_G5[:], AF.Silu, inc="a")
                P.wait("vector", "a")
                P.op("vector", "tensor_mul",
                     H5[:, ot * 1024 + tb2 * 512: ot * 1024 + tb2 * 512 + 512],
                     SG5[:], PS_U5[:], inc="v")
        evmk = {}
        for fg in range(4):
            tprev = None
            for ic in range(NI0):
                hf = ic % 2
                if tprev is not None:
                    P.wait("sync", "t", tprev)
                else:
                    P.wait("sync", "t")
                wd_ = P.dma("sync", WD5[:, hf * 512:(hf + 1) * 512],
                            h["wdb"][ic][:, fg * 512:(fg + 1) * 512],
                            stream="wd5" + str(hf))
                P.wait("tensor", "dS_wd5" + str(hf), wd_)
                for t8 in range(8):
                    if ic == 0 and (fg - 1, t8) in evmk:
                        P.wait("tensor", "v", evmk[(fg - 1, t8)])
                    elif ic == 0 and t8 == 0:
                        P.wait("tensor", "v")
                    outap = k.ps[t8][:] if t8 < 6 else k.ps[t8][:].bitcast(F32)
                    tprev = P.op("tensor", "matmul", outap,
                         H5[:, ic * 1024 + t8 * 128: ic * 1024 + (t8 + 1) * 128],
                         WD5[:, hf * 512:(hf + 1) * 512],
                         start=(ic == 0), stop=(ic == NI0 - 1),
                         inc="t" if t8 == 7 else None)
            P.wait("vector", "t")
            P.wait("vector", "dG_p5")
            for t8 in range(8):
                inap = k.ps[t8][:] if t8 < 6 else k.ps[t8][:].bitcast(F32)
                evmk[(fg, t8)] = P.op("vector", "tensor_copy",
                     P5E[:, t8 * 512:(t8 + 1) * 512],
                     inap, inc="v")
            P.wait("gpsimd", "v")
            P.dma("gpsimd",
                  h["pre2"].rearrange("(tt p) f -> p tt f", p=128)[
                      :, th * 8:(th + 1) * 8, fg * 512:(fg + 1) * 512],
                  P5E[:].rearrange("p (tt d) -> p tt d", d=512), stream="p5")
    P.barrier()


def emit_p7(k: K):
    if not k.fused:
        emit_fm(k, "x2", k.s["XFM"][:, 0:2048])
    if k.fused and k.pgw is not None and k.pgw.cv8 is not None:
        k.P.wait("sync", "dG_cv8", k.pgw.cv8)
    P, h, s, ps = k.P, k.h, k.s, k.ps
    RFM, AFM = s["RFM"], s["XFM"]
    WB7 = s["WB7"]
    QPH, KPH, VPH, RKV, RKS = s["QPH"], s["KPH"], s["VPH"], s["RKV"], s["RKS"]
    TMP7, E7 = s["TMP7"], s["E7"]
    DEC1, DECB = s["DEC1"], s["DECB"]
    DEN7 = s["DEN7"]
    OUT7 = s["XT4"]
    onesf, onesrf = s["onesf"], s["onesrf"]
    PS_A, PS_Bc, PS_R = ps[0], ps[1], ps[2]

    P.op("vector", "memset", onesf[:], 1.0)
    P.op("vector", "memset", s["onesb"][:], 1.0)
    P.op("vector", "memset", onesrf[:], 1.0, inc="v")
    cd = P.dma("sync", DEC1[:], h["decay"], stream="c")
    P.wait("scalar", "dS_c", cd)
    P.op("scalar", "activation", DEC1[:], DEC1[:], AF.Sigmoid, inc="a")
    P.wait("tensor", "a")
    P.wait("tensor", "v")
    P.op("tensor", "matmul", PS_R[:, 0:16], onesrf[:], DEC1[:],
         start=True, stop=True, inc="t")
    P.wait("vector", "t")
    P.op("vector", "tensor_copy", DECB[:], PS_R[:, 0:16], inc="v")

    tmark = {}
    nsl = 0
    q_done = {}
    s2v = {}

    def stage1(fb):
        nonlocal nsl
        par = fb % 2
        QPHh = QPH[:, par * 2048:(par + 1) * 2048]
        KPHh = KPH[:, par * 2048:(par + 1) * 2048]
        VPHh = VPH[:, par * 2048:(par + 1) * 2048]
        if fb - 2 in s2v:
            P.wait("vector", "v", s2v[fb - 2])
        for slot, (dst, phi) in enumerate(
                ((QPHh, True), (KPHh, True), (VPHh, False))):
            hf = nsl % 2
            if nsl >= 2:
                P.wait("sync" if k.fused else "gpsimd", "t", tmark[nsl - 2])
            if k.fused:
                wv_ = P.dma("sync", WB7[:, hf * 2048:(hf + 1) * 2048],
                            h["wqkv8"][slot, fb], stream="w7" + str(hf))
                P.wait("tensor", "dS_w7" + str(hf), wv_)
            else:
                wv_ = P.dma("gpsimd", WB7[:, hf * 2048:(hf + 1) * 2048],
                            h["wqkv"][slot, fb], stream="w7" + str(hf))
                P.wait("tensor", "dG_w7" + str(hf), wv_)
            for tb in range(4):
                tpar = tb % 2
                PS_Ax = ps[0] if tpar == 0 else ps[4]
                if nsl == 0 and tb < 2:
                    P.wait("tensor", "v")
                else:
                    P.wait("tensor", "v", max(P.sems["v"].n - 1, 0))
                for c in range(NCH):
                    tmark[nsl] = P.op("tensor", "matmul", PS_Ax[:],
                         WB7[:, hf * 2048 + c * 128: hf * 2048 + (c + 1) * 128],
                         RFM[:, c * 2048 + tb * 512: c * 2048 + (tb + 1) * 512],
                         start=(c == 0), stop=(c == NCH - 1),
                         inc="t" if c == NCH - 1 else None)
                dslice = dst[:, tb * 512:(tb + 1) * 512]
                if phi:
                    P.wait("vector", "t")
                    P.op("vector", "tensor_scalar_min", TMP7[:], PS_Ax[:], 0.0,
                         inc="vb")
                    P.wait("scalar", "vb")
                    P.op("scalar", "activation", E7[:], TMP7[:], AF.Exp, inc="a")
                    P.wait("vector", "a")
                    P.op("vector", "scalar_tensor_tensor", dslice, PS_Ax[:], 0.0,
                         E7[:], ALU.max, ALU.add, inc="v")
                else:
                    P.wait("vector", "t")
                    P.op("vector", "tensor_copy", dslice, PS_Ax[:], inc="v")
            nsl += 1
        q_done[fb] = P.sems["v"].n

    def stage2(fb):
        par = fb % 2
        QPHh = QPH[:, par * 2048:(par + 1) * 2048]
        KPHh = KPH[:, par * 2048:(par + 1) * 2048]
        VPHh = VPH[:, par * 2048:(par + 1) * 2048]
        RKVh = RKV[:, par * 2048:(par + 1) * 2048]
        RKSh = RKS[:, par * 2048:(par + 1) * 2048]
        P.wait("vector", "v", q_done[fb])
        P.op("vector", "tensor_mul", VPHh, KPHh, VPHh, inc="v")
        P.wait("vector", "v")
        P.op("vector", "tensor_tensor_scan", RKVh,
             DECB[:, fb:fb + 1].broadcast_to([128, T]), VPHh, 0.0,
             ALU.mult, ALU.add, inc="v")
        P.wait("vector", "v")
        P.op("vector", "tensor_tensor_scan", RKSh,
             DECB[:, fb:fb + 1].broadcast_to([128, T]), KPHh, 0.0,
             ALU.mult, ALU.add, inc="v")
        P.wait("vector", "v")
        P.op("vector", "tensor_mul", VPHh, QPHh, RKSh, inc="v")
        vpp = P.sems["v"].n
        for tb in range(4):
            P.wait("tensor", "v", vpp)
            if tb > 0:
                P.wait("tensor", "v")
            P.op("tensor", "matmul", PS_R[0:1, 0:512], s["onesb"][:],
                 VPHh[:, tb * 512:(tb + 1) * 512], start=True, stop=True,
                 inc="t")
            P.wait("vector", "t")
            P.op("vector", "tensor_copy", DEN7[:, tb * 512:(tb + 1) * 512],
                 PS_R[0:1, 0:512], inc="v")
        P.wait("vector", "v")
        P.op("vector", "tensor_scalar_max", DEN7[:], DEN7[:], 1e-6, inc="v")
        P.wait("vector", "v")
        P.op("vector", "reciprocal", DEN7[:], DEN7[:], inc="v")
        vden = P.sems["v"].n
        P.wait("vector", "v")
        P.op("vector", "tensor_mul", RKVh, QPHh, RKVh, inc="v")
        vnum = P.sems["v"].n
        for tb in range(4):
            P.wait("tensor", "v", vden)
            if tb > 0:
                P.wait("tensor", "v")
            P.op("tensor", "matmul", PS_Bc[:], onesrf[:],
                 DEN7[:, tb * 512:(tb + 1) * 512], start=True, stop=True,
                 inc="t")
            P.wait("vector", "t")
            P.wait("vector", "v", vnum)
            P.op("vector", "tensor_mul",
                 AFM[:, fb * 2048 + tb * 512: fb * 2048 + (tb + 1) * 512],
                 RKVh[:, tb * 512:(tb + 1) * 512], PS_Bc[:], inc="v")
            P.wait("tensor", "v")
        s2v[fb] = P.sems["v"].n

    for fb in range(NH + 1):
        if fb < NH:
            stage1(fb)
        if fb >= 1:
            stage2(fb - 1)
    if k.pgw is not None:
        k.pgw.flush()
    P.barrier()
    WOB = s["WB7"]
    for jh in range(2):
        for fo in range(4):
            tprev = None
            for c in range(NCH):
                wslice = WOB[:, (c % 2) * 512:(c % 2) * 512 + 512]
                if tprev is not None:
                    P.wait("sync" if k.fused else "gpsimd", "t", tprev)
                else:
                    P.wait("sync" if k.fused else "gpsimd", "t")
                if k.fused:
                    wv_ = P.dma("sync", wslice, h["wout8"][fo, c],
                                stream="wo7" + str(c % 2))
                    P.wait("tensor", "dS_wo7" + str(c % 2), wv_)
                else:
                    wv_ = P.dma("gpsimd", wslice, h["wout"][fo, c],
                                stream="wo7" + str(c % 2))
                    P.wait("tensor", "dG_wo7" + str(c % 2), wv_)
                if c == 0:
                    P.wait("tensor", "v")
                for j8 in range(8):
                    j = jh * 8 + j8
                    if j8 < 6:
                        outap = k.ps[j8][:]
                    else:
                        outap = k.ps[j8][:].bitcast(F32)
                    tprev = P.op("tensor", "matmul", outap,
                         AFM[:, c * 2048 + j * 128: c * 2048 + (j + 1) * 128],
                         wslice, start=(c == 0), stop=(c == NCH - 1),
                         inc="t" if j8 == 7 else None)
            P.wait("vector", "t")
            for hv in range(2):
                P.wait("vector", "dG_o7")
                for j8 in range(hv * 4, hv * 4 + 4):
                    inap = k.ps[j8][:] if j8 < 6 else k.ps[j8][:].bitcast(F32)
                    P.op("vector", "tensor_copy",
                         OUT7[:, (j8 % 4) * 512:(j8 % 4 + 1) * 512],
                         inap, inc="v" if j8 % 4 == 3 else None)
                P.wait("gpsimd", "v")
                P.dma("gpsimd",
                      h["pre3"].rearrange("(tt p) f -> p tt f", p=128)[
                          :, jh * 8 + hv * 4: jh * 8 + (hv + 1) * 4,
                          fo * 512:(fo + 1) * 512],
                      OUT7[:, 0:2048].rearrange("p (tt d) -> p tt d", d=512),
                      stream="o7")
    P.barrier()


def emit_p8(k: K):
    if not k.fused:
        emit_fm(k, "x3", k.s["H8"][:, 0:2048])
    if k.fused and k.pgw is not None and k.pgw.cv8 is not None:
        k.P.wait("sync", "dG_cv8", k.pgw.cv8)
    P, h, s, ps = k.P, k.h, k.s, k.ps
    RFM, H8 = s["RFM"], s["H8"]
    WGB5, WUB5, WD5 = s["WGB5"], s["WUB5"], s["WD5"]
    SG5, P5E = s["SG5"], s["P5E"]
    GS8, US8, DSB = s["GS8"], s["US8"], s["DSB"]
    DS1 = s["P5E"][0:1, 0:2048]
    onesrf = s["onesrf"]
    PS_G5, PS_U5, PS_D0, PS_D1, PS_R = ps[0], ps[1], ps[2], ps[3], ps[4]
    P.op("vector", "memset", onesrf[:], 1.0, inc="v")
    P.dma("sync", GS8[:], h["gs"], stream="c")
    P.dma("sync", US8[:], h["us"], stream="c")
    cd = P.dma("sync", DS1, h["dsrow"], stream="c")
    P.wait("tensor", "dS_c", cd)
    P.wait("tensor", "v")
    for tb in range(4):
        P.op("tensor", "matmul", PS_R[:], onesrf[:],
             DS1[0:1, tb * 512:(tb + 1) * 512], start=True, stop=True, inc="t")
        P.wait("vector", "t")
        P.op("vector", "tensor_copy", DSB[:, tb * 512:(tb + 1) * 512], PS_R[:],
             inc="v")
        P.wait("tensor", "v")
    P.wait("scalar", "dS_c", cd)
    for th in range(2):
        tmark = {}
        for ot in range(NI1):
            hf = ot % 2
            if ot >= 2:
                P.wait("sync" if k.fused else "gpsimd", "t", tmark[ot - 2])
                if k.fused:
                    P.wait("scalar", "t", tmark[ot - 2])
            if k.fused:
                g_ = P.dma("sync", WGB5[:, hf * 2048:(hf + 1) * 2048],
                           h["gw8"][ot], stream="wg8" + str(hf))
                u_ = P.dma("scalar", WUB5[:, hf * 2048:(hf + 1) * 2048],
                           h["uw8"][ot], stream="wu8" + str(hf))
                P.wait("tensor", "dS_wg8" + str(hf), g_)
                P.wait("tensor", "dA_wu8" + str(hf), u_)
            else:
                P.dma("gpsimd", WGB5[:, hf * 2048:(hf + 1) * 2048], h["gwT"][ot],
                      stream="wg8" + str(hf))
                wv_ = P.dma("gpsimd", WUB5[:, hf * 2048:(hf + 1) * 2048],
                            h["uwT"][ot], stream="wg8" + str(hf))
                P.wait("tensor", "dG_wg8" + str(hf), wv_)
            for tb2 in range(2):
                tc = th * 1024 + tb2 * 512
                par = tb2 % 2
                PS_G5 = k.ps[0] if par == 0 else k.ps[2]
                PS_U5 = k.ps[1] if par == 0 else k.ps[3]
                if ot == 0:
                    P.wait("tensor", "a")
                    P.wait("tensor", "v")
                else:
                    P.wait("tensor", "a", max(P.sems["a"].n - 1, 0) if "a" in P.sems else 0)
                    P.wait("tensor", "v", max(P.sems["v"].n - 1, 0) if "v" in P.sems else 0)
                for c in range(NCH):
                    P.op("tensor", "matmul", PS_G5[:],
                         WGB5[:, hf * 2048 + c * 128: hf * 2048 + (c + 1) * 128],
                         RFM[:, c * 2048 + tc: c * 2048 + tc + 512],
                         start=(c == 0), stop=(c == NCH - 1))
                for c in range(NCH):
                    tmark[ot] = P.op("tensor", "matmul", PS_U5[:],
                         WUB5[:, hf * 2048 + c * 128: hf * 2048 + (c + 1) * 128],
                         RFM[:, c * 2048 + tc: c * 2048 + tc + 512],
                         start=(c == 0), stop=(c == NCH - 1),
                         inc="t" if c == NCH - 1 else None)
                P.wait("scalar", "t")
                P.wait("scalar", "v")
                P.op("scalar", "activation", SG5[:], PS_G5[:], AF.Silu,
                     scale=GS8[:, ot:ot + 1], inc="a")
                P.wait("vector", "a")
                P.op("vector", "scalar_tensor_tensor",
                     H8[:, ot * 1024 + tb2 * 512: ot * 1024 + tb2 * 512 + 512],
                     PS_U5[:], US8[:, ot:ot + 1], SG5[:], ALU.mult, ALU.mult,
                     inc="v")
        evmk = {}
        for fg in range(4):
            tprev = None
            for ic in range(NI1):
                hf = ic % 2
                if tprev is not None:
                    P.wait("sync" if k.fused else "gpsimd", "t", tprev)
                else:
                    P.wait("sync" if k.fused else "gpsimd", "t")
                if k.fused:
                    wd_ = P.dma("sync", WD5[:, hf * 512:(hf + 1) * 512],
                                h["dw8"][ic][:, fg * 512:(fg + 1) * 512],
                                stream="wd8" + str(hf))
                    P.wait("tensor", "dS_wd8" + str(hf), wd_)
                else:
                    wd_ = P.dma("gpsimd", WD5[:, hf * 512:(hf + 1) * 512],
                                h["dwT"][ic][:, fg * 512:(fg + 1) * 512],
                                stream="wd8" + str(hf))
                    P.wait("tensor", "dG_wd8" + str(hf), wd_)
                for t8 in range(8):
                    if ic == 0 and (fg - 1, t8) in evmk:
                        P.wait("tensor", "v", evmk[(fg - 1, t8)])
                    elif ic == 0 and t8 == 0:
                        P.wait("tensor", "v")
                    outap = k.ps[t8][:] if t8 < 6 else k.ps[t8][:].bitcast(F32)
                    tprev = P.op("tensor", "matmul", outap,
                         H8[:, ic * 1024 + t8 * 128: ic * 1024 + (t8 + 1) * 128],
                         WD5[:, hf * 512:(hf + 1) * 512],
                         start=(ic == 0), stop=(ic == NI1 - 1),
                         inc="t" if t8 == 7 else None)
            P.wait("vector", "t")
            P.wait("vector", "dG_p8")
            for t8 in range(8):
                inap = k.ps[t8][:] if t8 < 6 else k.ps[t8][:].bitcast(F32)
                evmk[(fg, t8)] = P.op("vector", "tensor_mul",
                     P5E[:, t8 * 512:(t8 + 1) * 512],
                     inap, DSB[:, fg * 512:(fg + 1) * 512], inc="v")
            P.wait("gpsimd", "v")
            P.dma("gpsimd",
                  h["pre4"].rearrange("(tt p) f -> p tt f", p=128)[
                      :, th * 8:(th + 1) * 8, fg * 512:(fg + 1) * 512],
                  P5E[:].rearrange("p (tt d) -> p tt d", d=512), stream="p8")
    P.barrier()


def alloc_sbuf(nc, es, names, suffix=""):
    SPECS = {
        "XFM": ((128, NCH * 2048), BF16),
        "RFM": ((128, NCH * 2048), BF16),
        "XF32": ((128, 8192), F32),
        "XT": ((128, 2048), F32),
        "XBF": ((128, 2048), BF16),
        "WF": ((128, 4096), F32),
        "WBF": ((128, 4096), BF16),
        "QT": ((128, 2048), F32),
        "SQ": ((128, 2048), F32),
        
        "KVBF": ((128, 2048), BF16),
        
        "NORMS2": ((128, 256), F32),
        "identb": ((128, 128), BF16),
        "identf": ((128, 128), F32),
        "onesf": ((128, 1), F32),
        "bq": ((128, NCH), F32),
        "bk": ((128, NCH), F32),
        "bv": ((128, NCH), F32),
        "MASK": ((128, 256), F32),
        "MASKB": ((128, 256), BF16),
        "INVM": ((128, 256), F32),
        "RANK": ((128, 256), F32),
        "CMP": ((128, 256), BF16),
        "LOa": ((1, 16), F32), "LOb": ((1, 16), F32),
        "HIa": ((1, 16), F32), "HIb": ((1, 16), F32),
        "MID": ((1, 16), F32), "MIDX": ((1, 256), F32),
        "CNT16": ((1, 16), F32), "GE": ((1, 16), I32), "TT16": ((1, 16), F32),
        "M1": ((128, 16), F32), "HROW": ((1, 16), F32), "H16C": ((16, 1), F32),
        "TOTS": ((1, 256), F32), "TOTJ": ((16, 16), F32),
        "OFFJS": ((16, 16), F32), "OFFROW": ((1, 256), F32),
        "PINCS": ((128, 256), F32),
        "onesb": ((128, 1), BF16), "onesrf": ((1, 128), F32),
        "triu": ((128, 128), BF16), "tris": ((16, 16), F32),
        "iota": ((128, SLACK), F32), "bemask": ((128, 256), F32),
        "QH": ((128, 4096), BF16), "KH": ((128, 4096), BF16),
        "VH": ((128, 4096), BF16), "QTM": ((128, 4096), BF16),
        "S2": ((128, 2 * NT * SLACK), BF16), "S2T": ((72, 2 * NT * 128), BF16),
        "PR": ((128, 768), BF16), "PT": ((128, 768), BF16),
        "CTXH": ((128, 4096), F32), "CTXO": ((128, 4096), F32),
        "MX": ((128, 2), F32), "NM": ((128, 2), F32), "RS": ((128, 2), F32),
        "SUMS": ((128, 2), F32),
        "QGB": ((128, SLACK), BF16), "GSF": ((72, 2048), F32),
        "GP": ((72, 2048), BF16), "GPT": ((128, NT * SLACK), BF16),
        "GCF": ((72, 128), BF16),
        "MXG": ((72, 1), F32), "NMG": ((72, 1), F32), "RSG": ((72, 1), F32),
        "GSUM": ((72, 1), F32),
        "XT4": ((128, 4096), F32), "CT4": ((128, 4096), F32),
        "SQ4": ((128, 4096), F32), "X1T": ((128, 4096), F32),
        "XB4": ((128, 2048), BF16),
        "SM": ((128, 2), F32), "VARS": ((128, 2), F32), "RSTD": ((128, 2), F32),
        "EPS4": ((128, 1), F32),
        "IDX5": ((128, 4096), I32), "IDXB": ((128, 4096), BF16),
        "TMP5": ((128, 2048), BF16), "WACC": ((128, 4096), BF16),
        "LUT1": ((1, 16), F32), "LUTB": ((128, 16), F32),
        "PGones": ((1, 128), F32), "PG1": ((128, 1), F32),
        "NEGL": ((128, 16), F32), "TMPA": ((128, 2048), BF16),
        "TMPB": ((128, 4096), BF16),
        "H5": ((128, NI0 * 1024), BF16),
        "H8": ((128, NI1 * 1024), BF16),
        "WGB5": ((128, 4096), BF16), "WUB5": ((128, 4096), BF16),
        "WD5": ((128, 1024), BF16),
        "SG5": ((128, 512), F32), "P5E": ((128, 4096), F32),
        "WB7": ((128, 4096), BF16),
        "QPH": ((128, 4096), BF16), "KPH": ((128, 4096), BF16),
        "VPH": ((128, 4096), BF16), "RKV": ((128, 4096), BF16),
        "RKS": ((128, 4096), BF16),
        "TMP7": ((128, 512), F32), "E7": ((128, 512), F32),
        "DEC1": ((1, 16), F32), "DECB": ((128, 16), F32),
        "DEN7": ((1, 2048), F32),
        "OUT7": ((128, 4096), F32),
        "GS8": ((128, NI1), F32), "US8": ((128, NI1), F32),
        "DSB": ((128, 2048), F32),
    }
    s = {}
    for name in names:
        shape, dt = SPECS[name]
        s[name] = es.enter_context(nc.sbuf_tensor("sb_" + name + suffix, list(shape), dt))
    return s


SBUF_BY_PHASE = {
    "p01": ["XFM", "XF32", "XT", "WF", "WBF", "QT", "SQ", "KVBF",
            "NORMS2", "identb", "identf", "onesf", "bq", "bk", "bv"],
    "p23": ["NORMS2", "identb", "identf", "onesf", "onesb", "onesrf",
            "triu", "tris", "iota", "bemask",
            "MASK", "MASKB", "INVM", "RANK", "CMP",
            "LOa", "LOb", "HIa", "HIb", "MID", "MIDX", "CNT16", "GE", "TT16",
            "M1", "HROW", "H16C", "TOTS", "TOTJ", "OFFJS", "OFFROW", "PINCS",
            "QH", "KH", "VH", "QTM", "S2", "S2T", "PR", "PT", "CTXH", "CTXO",
            "MX", "NM", "RS", "SUMS", "QGB", "GSF", "GP", "GPT", "GCF",
            "MXG", "NMG", "RSG", "GSUM"],
    "ln1": ["XT4", "CT4", "SQ4", "X1T", "SM", "VARS", "RSTD", "EPS4"],
    "ln2": ["XT4", "CT4", "SQ4", "X1T", "SM", "VARS", "RSTD", "EPS4"],
    "ln3": ["XT4", "CT4", "SQ4", "X1T", "SM", "VARS", "RSTD", "EPS4"],
    "ln4": ["XT4", "CT4", "SQ4", "X1T", "SM", "VARS", "RSTD", "EPS4"],
    "pgw": ["IDX5", "IDXB", "TMP5", "WACC", "LUT1", "LUTB", "PGones", "PG1", "NEGL", "TMPA", "TMPB"],
    "p5":  ["RFM", "XT4", "identb", "H5", "WGB5", "WUB5", "WD5",
            "SG5", "P5E"],
    "p7":  ["RFM", "XT4", "identb", "XFM", "WB7", "QPH", "KPH", "VPH",
            "RKV", "RKS", "TMP7", "E7", "DEC1", "DECB", "DEN7",
            "onesf", "onesrf", "onesb"],
    "p8":  ["RFM", "XT4", "identb", "H8", "WGB5", "WUB5", "WD5",
            "SG5", "P5E", "GS8", "US8", "DSB", "onesrf"],
}


def alloc_psum(nc, es):
    ps = []
    for i in range(6):
        ps.append(es.enter_context(nc.psum_tensor(f"ps{i}", [128, 512], F32)))
    for i in range(2):
        ps.append(es.enter_context(nc.psum_tensor(f"psb{i}", [128, 1024], BF16)))
    return ps


EMITTERS = {"p01": emit_p01, "p23": emit_p23,
            "ln1": emit_ln1, "ln2": emit_ln2, "ln3": emit_ln3, "ln4": emit_ln4,
            "pgw": emit_pgw, "p5": emit_p5, "p7": emit_p7, "p8": emit_p8}


def build_launch(phases, fused=False):
    nc = bass.Bass("TRN2", target_bir_lowering=False)
    reads, writes = set(), set()
    for ph in phases:
        r, w = PHASE_IO[ph]
        reads |= set(r)
        writes |= set(w)
    hbm, input_names = {}, []
    for name, (shape, dt, src_) in HBM_SPECS.items():
        if name not in reads and name not in writes:
            continue
        if src_ in ("in", "const") and name in reads:
            kind = "ExternalInput"
            input_names.append(name)
        elif name in writes:
            if fused:
                kind = "ExternalOutput" if src_ == "out" else "Internal"
            else:
                kind = "ExternalOutput"
        else:
            kind = "ExternalInput"
            input_names.append(name)
        hbm[name] = nc.dram_tensor(name, list(shape), dt, kind=kind).ap()

    PERSIST = ("identb", "identf", "onesf", "onesb", "onesrf")
    with ExitStack() as es:
        ps = alloc_psum(nc, es)
        P = Prog(nc)
        if fused:
            persist_names = []
            for ph in phases:
                for n in SBUF_BY_PHASE[ph]:
                    if n in PERSIST and n not in persist_names:
                        persist_names.append(n)
            has_pgw = "pgw" in phases
            sbp = alloc_sbuf(nc, es, persist_names)
            pgwobj = None
            es_pgw = None
            if has_pgw:
                es_pgw = ExitStack()
                sb_pgw = alloc_sbuf(nc, es_pgw, SBUF_BY_PHASE["pgw"],
                                    suffix="_pgw")
                kp = K(nc, P, hbm, {**sbp, **sb_pgw}, ps)
                kp.fused = True
                pgwobj = Pgw(kp)
            for i, ph in enumerate(phases):
                with ExitStack() as es2:
                    names = [n for n in SBUF_BY_PHASE[ph]
                             if n not in PERSIST and n not in sbp]
                    if ph in ("p5", "p8"):
                        names = [n for n in names if n != "XT4"]
                    sb = dict(alloc_sbuf(nc, es2, names, suffix=f"_{i}"))
                    sb.update(sbp)
                    k = K(nc, P, hbm, sb, ps)
                    k.fused = True
                    k.pgw = pgwobj
                    EMITTERS[ph](k)
                if ph == "p23" and es_pgw is not None:
                    es_pgw.close()
                    es_pgw = None
                if ph == "p23" and fused:
                    sbp.update(alloc_sbuf(nc, es, ["RFM"], suffix="_r"))
            if es_pgw is not None:
                es_pgw.close()
        else:
            sb_names = []
            for ph in phases:
                for n in SBUF_BY_PHASE[ph]:
                    if n not in sb_names:
                        sb_names.append(n)
            sb = alloc_sbuf(nc, es, sb_names)
            k = K(nc, P, hbm, sb, ps)
            for ph in phases:
                EMITTERS[ph](k)
        for n in list(P.sems):
            if n.startswith("d") and "_" in n:
                P.wait("gpsimd", n, P.sems[n].n)
        block = es.enter_context(nc.Block())
        P.replay(block)
    return nc, input_names


def run_groups(shared, xs, groups, n_cores=8):
    results = [dict() for _ in range(n_cores)]
    nlaunch = 0
    for group in groups:
        group = [p for p in group if p in EMITTERS]
        if not group:
            continue
        nc, input_names = build_launch(group)
        in_maps = []
        for b in range(n_cores):
            m = {}
            for name in input_names:
                if name == "x":
                    m[name] = xs[b]
                elif name in shared:
                    m[name] = shared[name]
                else:
                    m[name] = results[b][name]
            in_maps.append(m)
        res = run_bass_kernel_spmd(nc, in_maps, core_ids=list(range(n_cores)))
        nlaunch += 1
        for b in range(n_cores):
            for kk, v in res.results[b].items():
                results[b][kk] = np.asarray(v)
    return results


ALL_PHASES = ["pgw", "p01", "p23", "ln1", "p5", "ln2", "p7", "ln3", "p8", "ln4"]


def kernel(**inputs):
    shared, xs = host_prepare(inputs)
    nc, input_names = build_launch(ALL_PHASES, fused=True)
    in_maps = []
    for b in range(8):
        m = {}
        for name in input_names:
            m[name] = xs[b] if name == "x" else shared[name]
        in_maps.append(m)
    res = run_bass_kernel_spmd(nc, in_maps, core_ids=list(range(8)))
    return np.stack([np.asarray(res.results[b]["y"]) for b in range(8)], axis=0)


# version 27
# speedup vs baseline: 1.1057x; 1.1057x over previous
import math
import traceback
from contextlib import ExitStack

DEBUG_LOC = True

import numpy as np
import ml_dtypes

import concourse.bass as bass
import concourse.mybir as mybir
from concourse.bass_utils import run_bass_kernel_spmd

F32 = mybir.dt.float32
BF16 = mybir.dt.bfloat16
I32 = mybir.dt.int32
AF = mybir.ActivationFunctionType
ALU = mybir.AluOpType
AX = mybir.AxisListType

T = 2048
DM = 2048
NH = 16
HD = 128
NT = 16
NCH = 16
I0 = 5504
I1 = 5632
NI0 = I0 // 128
NI1 = I1 // 128
SLACK = 72
SCALE = 1.0 / math.sqrt(HD)
NBIS = 38

ENGS = ("sync", "scalar", "vector", "tensor", "gpsimd")


class Sem:
    __slots__ = ("h", "n")
    def __init__(self, h):
        self.h = h
        self.n = 0


class Prog:

    def __init__(self, nc):
        self.nc = nc
        self.ops = {e: [] for e in ENGS}
        self.sems = {}

    def sem(self, name):
        if name not in self.sems:
            self.sems[name] = Sem(self.nc.alloc_semaphore(name))
        return self.sems[name]

    def n(self, name):
        return self.sem(name).n

    def op(self, eng, meth, *args, inc=None, inck=1, **kw):
        loc = traceback.extract_stack(limit=2)[0] if DEBUG_LOC else None
        if inc is not None:
            s = self.sem(inc)
            s.n += inck
            def fn(e, meth=meth, args=args, kw=kw, s=s, inck=inck, loc=loc):
                try:
                    getattr(e, meth)(*args, **kw).then_inc(s.h, inck)
                except Exception:
                    print(f"FAILED OP {meth} recorded at {loc.filename}:{loc.lineno}")
                    raise
            self.ops[eng].append(fn)
            return s.n
        def fn(e, meth=meth, args=args, kw=kw, loc=loc):
            try:
                getattr(e, meth)(*args, **kw)
            except Exception:
                print(f"FAILED OP {meth} recorded at {loc.filename}:{loc.lineno}")
                raise
        self.ops[eng].append(fn)
        return None

    def wait(self, eng, semname, val=None):
        s = self.sem(semname)
        if val is None:
            val = s.n
        if val <= 0:
            return
        def fn(e, s=s, val=val):
            e.wait_ge(s.h, val)
        self.ops[eng].append(fn)

    def dma(self, queue, out, in_, stream="m", **kw):
        semname = {"sync": "dS", "scalar": "dA", "gpsimd": "dG"}[queue] + "_" + stream
        return self.op(queue, "dma_start", inc=semname, inck=16, out=out, in_=in_, **kw)

    BG_STREAMS = ("dG_cv8",)

    def barrier(self):
        for n in list(self.sems):
            if (n.startswith("d") and "_" in n and self.sems[n].n > 0
                    and n not in self.BG_STREAMS):
                self.wait("gpsimd", n, self.sems[n].n)
        s = self.sem("bar")
        s.n += len(ENGS)
        target = s.n
        for e in ENGS:
            def fin(eng, s=s):
                eng.nop().then_inc(s.h, 1)
            self.ops[e].append(fin)
        for e in ENGS:
            def fw(eng, s=s, target=target):
                eng.wait_ge(s.h, target)
            self.ops[e].append(fw)

    def replay(self, block):
        for ename in ENGS:
            deco = getattr(block, ename)
            oplist = self.ops[ename]
            def section(e, oplist=oplist):
                for fn in oplist:
                    fn(e)
            deco(section)


HBM_SPECS = {
    "x":      ((T, DM), F32, "in"),
    "wq":     ((NCH, 128, DM), F32, "in"),
    "wk":     ((NCH, 128, DM), F32, "in"),
    "wv":     ((NCH, 128, DM), F32, "in"),
    "bq":     ((128, NCH), F32, "in"),
    "bk":     ((128, NCH), F32, "in"),
    "bv":     ((128, NCH), F32, "in"),
    "lut":    ((1, 16), F32, "in"),
    "gidxT":  ((NI0, 128, DM), I32, "in"),
    "uidxT":  ((NI0, 128, DM), I32, "in"),
    "didxT":  ((I0, DM), I32, "in"),
    "wqkv":   ((3, NH, 128, DM), F32, "in"),
    "wout":   ((4, NCH, 128, 512), F32, "in"),
    "decay":  ((1, NH), F32, "in"),
    "gwT":    ((NI1, 128, DM), F32, "in"),
    "uwT":    ((NI1, 128, DM), F32, "in"),
    "dwT":    ((NI1, 128, DM), F32, "in"),
    "gw8":    ((NI1, 128, DM), BF16, "scratch"),
    "uw8":    ((NI1, 128, DM), BF16, "scratch"),
    "dw8":    ((NI1, 128, DM), BF16, "scratch"),
    "wqkv8":  ((3, NH, 128, DM), BF16, "scratch"),
    "wout8":  ((4, NCH, 128, 512), BF16, "scratch"),
    "gs":     ((128, NI1), F32, "in"),
    "us":     ((128, NI1), F32, "in"),
    "dsrow":  ((1, DM), F32, "in"),
    "identb": ((128, 128), BF16, "const"),
    "identf": ((128, 128), F32, "const"),
    "triu":   ((128, 128), BF16, "const"),
    "tris":   ((16, 16), F32, "const"),
    "iota":   ((128, SLACK), F32, "const"),
    "bemask": ((128, 256), F32, "const"),
    "negl":   ((128, 16), F32, "const"),
    "qfm":    ((DM, T), BF16, "scratch"),
    "kfm":    ((DM, T), BF16, "scratch"),
    "vtm":    ((T, DM), BF16, "scratch"),
    "norms":  ((128, 256), F32, "scratch"),
    "ctx":    ((T, DM), F32, "scratch"),
    "x1":     ((T, DM), F32, "scratch"),
    "pre2":   ((T, DM), F32, "scratch"),
    "x2":     ((T, DM), F32, "scratch"),
    "x3":     ((T, DM), F32, "scratch"),
    "wgb":    ((NI0, 128, DM), BF16, "scratch"),
    "wub":    ((NI0, 128, DM), BF16, "scratch"),
    "wdb":    ((NI0, 128, DM), BF16, "scratch"),
    "pre3":   ((T, DM), F32, "scratch"),
    "pre4":   ((T, DM), F32, "scratch"),
    "sc256a": ((1, 256), F32, "scratch"),
    "sc256b": ((16, 16), F32, "scratch"),
    "y":      ((T, DM), F32, "out"),
}

PHASE_IO = {
    "p01": (["x", "wq", "wk", "wv", "bq", "bk", "bv", "identb", "identf"],
            ["qfm", "kfm", "vtm", "norms"]),
    "p23": (["qfm", "kfm", "vtm", "norms", "identb", "identf", "triu", "tris",
             "iota", "bemask"], ["ctx", "sc256a", "sc256b"]),
    "ln1": (["x", "ctx"], ["x1"]),
    "pgw": (["lut", "gidxT", "uidxT", "didxT", "negl", "gwT", "uwT", "dwT",
             "wqkv", "wout"],
            ["wgb", "wub", "wdb", "gw8", "uw8", "dw8", "wqkv8", "wout8"]),
    "p5":  (["wgb", "wub", "wdb", "x1", "identb"], ["pre2"]),
    "ln2": (["x1", "pre2"], ["x2"]),
    "p7":  (["x2", "wqkv", "wout", "decay", "identb"], ["pre3"]),
    "ln3": (["x2", "pre3"], ["x3"]),
    "p8":  (["x3", "gwT", "uwT", "dwT", "gs", "us", "dsrow", "identb",
             "gw8", "uw8", "dw8"], ["pre4"]),
    "ln4": (["x3", "pre4"], ["y"]),
}
LN_ARGS = {"ln1": ("x", "ctx", "x1"), "ln2": ("x1", "pre2", "x2"),
           "ln3": ("x2", "pre3", "x3"), "ln4": ("x3", "pre4", "y")}
FM_SRC = {"p5": "x1", "p7": "x2", "p8": "x3"}
LAUNCH_GROUPS = [["p01"], ["p23"], ["ln1"], ["pgw"], ["p5"], ["ln2"],
                 ["p7"], ["ln3"], ["p8"], ["ln4"]]


def host_prepare(inputs):
    f = np.ascontiguousarray
    d = {}
    def tile_w(w, not_, w128=128):
        return f(w.reshape(NCH, 128, not_, w128).transpose(2, 1, 0, 3)
                 .reshape(not_, 128, NCH * w128))
    d["wq"] = tile_w(inputs["L0_attn_Wq"], NCH)
    d["wk"] = tile_w(inputs["L0_attn_Wk"], NCH)
    d["wv"] = tile_w(inputs["L0_attn_Wv"], NCH)
    d["bq"] = f(inputs["L0_attn_bq"].reshape(NCH, 128).T)
    d["bk"] = f(inputs["L0_attn_bk"].reshape(NCH, 128).T)
    d["bv"] = f(inputs["L0_attn_bv"].reshape(NCH, 128).T)
    d["lut"] = f(inputs["L0_ffn_lut"].reshape(1, 16))
    d["gidxT"] = tile_w(inputs["L0_ffn_gate_idx"].T, NI0)
    d["uidxT"] = tile_w(inputs["L0_ffn_up_idx"].T, NI0)
    d["didxT"] = f(inputs["L0_ffn_down_idx"].T)
    d["wqkv"] = f(inputs["L1_attn_w_qkv"].reshape(NCH, 128, 3, NH, 128)
                  .transpose(2, 3, 1, 0, 4).reshape(3, NH, 128, DM))
    d["wout"] = f(inputs["L1_attn_w_out"].reshape(NCH, 128, 4, 512)
                  .transpose(2, 0, 1, 3))
    d["decay"] = f(inputs["L1_attn_decay_param"].reshape(1, NH))
    d["gwT"] = tile_w(inputs["L1_ffn_gate_w"].T, NI1)
    d["uwT"] = tile_w(inputs["L1_ffn_up_w"].T, NI1)
    d["dwT"] = f(inputs["L1_ffn_down_w"].T.reshape(NI1, 128, DM))
    d["gs"] = f(inputs["L1_ffn_gate_s"].reshape(NI1, 128).T)
    d["us"] = f(inputs["L1_ffn_up_s"].reshape(NI1, 128).T)
    d["dsrow"] = f(inputs["L1_ffn_down_s"].reshape(1, DM))
    d["identb"] = np.eye(128, dtype=ml_dtypes.bfloat16)
    d["identf"] = np.eye(128, dtype=np.float32)
    d["triu"] = np.triu(np.ones((128, 128), np.float32)).astype(ml_dtypes.bfloat16)
    d["tris"] = np.triu(np.ones((16, 16), np.float32), 1)
    d["iota"] = np.broadcast_to(np.arange(SLACK, dtype=np.float32), (128, SLACK)).copy()
    be = np.zeros((128, 256), np.float32)
    be[0, 0:16] = 1.0
    be[127, 240:256] = 1.0
    d["bemask"] = be
    d["negl"] = np.broadcast_to(-np.arange(16, dtype=np.float32), (128, 16)).copy()
    xs = [np.ascontiguousarray(np.asarray(inputs["L0_attn_hidden_states"])[b])
          for b in range(8)]
    return d, xs


class Pgw:

    def __init__(self, k):
        self.k = k
        self.vmark = {}
        self.sttmark = {}
        self.accmark = {}
        h = k.h
        self.jobs = []
        for src_n, dst_n, tiled in (("gidxT", "wgb", True), ("uidxT", "wub", True),
                                    ("didxT", "wdb", False)):
            for c in range(NI0):
                src_ap = h[src_n][c] if tiled else h[src_n][c * 128:(c + 1) * 128, :]
                self.jobs.append((src_ap, h[dst_n][c]))
        self.i = 0
        self.half = 0
        self.done_prolog = False

    def emit_cv8(self):
        k = self.k
        P, h = k.P, k.h
        for srcn, dstn in (("gwT", "gw8"), ("uwT", "uw8"), ("dwT", "dw8")):
            for q in range(4):
                self.cv8 = P.dma("gpsimd", h[dstn][q * 11:(q + 1) * 11],
                                 h[srcn][q * 11:(q + 1) * 11], stream="cv8")
        for slot in range(3):
            self.cv8 = P.dma("gpsimd", h["wqkv8"][slot], h["wqkv"][slot],
                             stream="cv8")
        for fo in range(4):
            self.cv8 = P.dma("gpsimd", h["wout8"][fo], h["wout"][fo],
                             stream="cv8")

    def prolog(self):
        k = self.k
        P, s = k.P, k.s
        self.cv8 = None
        P.op("vector", "memset", s["PGones"][:], 1.0)
        P.op("vector", "memset", s["PG1"][:], 1.0, inc="vg")
        cl = P.dma("sync", s["LUT1"][:], k.h["lut"], stream="cg")
        cn = P.dma("sync", s["NEGL"][:], k.h["negl"], stream="cg")
        P.wait("tensor", "dS_cg", cn)
        P.wait("scalar", "dS_cg", cn)
        P.wait("tensor", "vg", P.sems["vg"].n)
        P.op("tensor", "matmul", k.ps[5][0:128, 0:16], s["PGones"][:], s["LUT1"][:],
             start=True, stop=True, inc="tg")
        P.wait("vector", "tg", P.sems["tg"].n)
        P.op("vector", "tensor_copy", s["LUTB"][:], k.ps[5][:, 0:16], inc="vg")
        self.done_prolog = True

    def step(self, n=1):
        k = self.k
        P, s = k.P, k.s
        IDX5, IDXB, TMP5, WACC, LUTB = (s["IDX5"], s["IDXB"], s["TMP5"],
                                        s["WACC"], s["LUTB"])
        for _ in range(n):
            if self.i >= len(self.jobs):
                return
            nblk = self.i
            hf = nblk % 2
            wslc = WACC[:, hf * 2048:(hf + 1) * 2048]
            islc = IDXB[:, hf * 2048:(hf + 1) * 2048]
            if self.half == 0:
                src_ap, dst_ap = self.jobs[nblk]
                h4 = nblk % 2
                if nblk >= 2:
                    P.wait("sync", "vg", self.vmark[nblk - 2])
                wi = P.dma("sync", IDX5[:, h4 * 2048:(h4 + 1) * 2048], src_ap,
                           stream="ixg" + str(h4))
                P.wait("vector", "dS_ixg" + str(h4), wi)
                P.wait("vector", "dG_wog" + str(hf))
                self.vmark[nblk] = P.op("vector", "tensor_copy", islc,
                                        IDX5[:, h4 * 2048:(h4 + 1) * 2048],
                                        inc="vg")
                P.wait("vector", "vg", P.sems["vg"].n)
                P.op("vector", "tensor_scalar", wslc, islc, 0.0,
                     LUTB[:, 0:1], ALU.is_equal, ALU.mult, inc="vg")
                self._tree(P, islc, wslc, LUTB, (1, 2, 3), s)
                self._tree(P, islc, wslc, LUTB, (4, 5, 6, 7), s)
                self.half = 1
            else:
                _, dst_ap = self.jobs[nblk]
                self._tree(P, islc, wslc, LUTB, (8, 9, 10, 11), s)
                self._tree(P, islc, wslc, LUTB, (12, 13, 14, 15), s)
                P.wait("gpsimd", "vg", P.sems["vg"].n)
                P.dma("gpsimd", dst_ap, wslc, stream="wog" + str(hf))
                self.half = 0
                self.i += 1


    def _tree(self, P, islc, wslc, LUTB, levels, s):
        RB = [s["TMP5"][:, 0:2048], s["TMPA"][:, 0:2048],
              s["TMPB"][:, 0:2048], s["TMPB"][:, 2048:4096]]
        P.wait("vector", "vg", P.sems["vg"].n)
        for i, l in enumerate(levels):
            P.op("vector", "tensor_scalar", RB[i], islc, float(l),
                 LUTB[:, l:l + 1], ALU.is_equal, ALU.mult, inc="vg")
        P.wait("vector", "vg", P.sems["vg"].n)
        if len(levels) == 4:
            P.op("vector", "tensor_add", RB[0], RB[0], RB[1], inc="vg")
            P.op("vector", "tensor_add", RB[2], RB[2], RB[3], inc="vg")
            P.wait("vector", "vg", P.sems["vg"].n)
            P.op("vector", "tensor_add", RB[0], RB[0], RB[2], inc="vg")
        else:
            P.op("vector", "tensor_add", RB[0], RB[0], RB[1], inc="vg")
            P.wait("vector", "vg", P.sems["vg"].n)
            P.op("vector", "tensor_add", RB[0], RB[0], RB[2], inc="vg")
        P.wait("vector", "vg", P.sems["vg"].n)
        P.op("vector", "tensor_add", wslc, wslc, RB[0], inc="vg")
        P.wait("vector", "vg", P.sems["vg"].n)

    def flush(self):
        self.step(2 * (len(self.jobs) - self.i) + self.half)



class K:
    def __init__(self, nc, P, h, s, ps):
        self.nc, self.P, self.h, self.s, self.ps = nc, P, h, s, ps
        self.pgw = None
        self.fused = False

    def pgw_step(self, n=1):
        if self.pgw is not None:
            self.pgw.step(n)


def dma_wtile(P, queue, dst_ap, W, c, fb, fbw=128):
    return P.dma(queue, dst_ap, W[c * 128:(c + 1) * 128, fb * fbw:(fb + 1) * fbw])


def emit_p01(k: K):
    P, h, s, ps = k.P, k.h, k.s, k.ps
    XFM, XF32 = s["XFM"], s["XF32"]
    XT, XBF = s["XT"], s["KVBF"]
    WF, WBF = s["WF"], s["WBF"]
    QT, SQ = s["QT"], s["SQ"]
    KVBF = s["KVBF"]
    QBF = KVBF[:, 0:2048]
    VF = s["QT"][:, 0:256].bitcast(BF16)
    NORMS2 = s["NORMS2"]
    idb, idf, onesf = s["identb"], s["identf"], s["onesf"]
    BQ, BK, BV = s["bq"], s["bk"], s["bv"]
    PS_MM, PS_TR, PS_NRM = ps[0], ps[1], ps[2]
    PS_B = ps[6]

    P.op("vector", "memset", onesf[:], 1.0, inc="v")
    for cn in ("identb", "identf", "bq", "bk", "bv"):
        P.dma("sync", s[cn][:], h[cn], stream="c")
    P.wait("tensor", "dS_c")
    P.wait("scalar", "dS_c")
    xr = h["x"]

    for j in range(NT):
        P.wait("sync", "v")
        P.dma("sync", XT[:], xr[j * 128:(j + 1) * 128, :], stream="x")
        P.wait("vector", "dS_x")
        P.wait("vector", "t")
        P.op("vector", "tensor_copy", XBF[:], XT[:], inc="v")
        P.wait("tensor", "v")
        for half in range(2):
            P.wait("tensor", "v")
            for cc in range(8):
                c = half * 8 + cc
                P.op("tensor", "transpose", PS_B[:, cc * 128:(cc + 1) * 128],
                     XBF[:, c * 128:(c + 1) * 128], idb[:],
                     inc="t" if cc == 7 else None)
            P.wait("vector", "t")
            for cc in range(8):
                c = half * 8 + cc
                P.op("vector", "tensor_copy",
                     XFM[:, c * 2048 + j * 128: c * 2048 + (j + 1) * 128],
                     PS_B[:, cc * 128:(cc + 1) * 128],
                     inc="v" if cc == 7 else None)
    P.barrier()

    for tb in range(4):
        for sub in range(4):
            j = tb * 4 + sub
            P.wait("sync", "t")
            P.dma("sync", XT[:], xr[j * 128:(j + 1) * 128, :], stream="x")
            P.wait("tensor", "dS_x")
            for half in range(4):
                P.wait("tensor", "v")
                for cc in range(4):
                    c = half * 4 + cc
                    P.op("tensor", "matmul", PS_TR[:, cc * 128:(cc + 1) * 128],
                         XT[:, c * 128:(c + 1) * 128], idf[:],
                         start=True, stop=True,
                         inc="t" if cc == 3 else None)
                P.wait("vector", "t")
                for cc in range(4):
                    c = half * 4 + cc
                    P.op("vector", "tensor_copy",
                         XF32[:, c * 512 + sub * 128: c * 512 + sub * 128 + 128],
                         PS_TR[:, cc * 128:(cc + 1) * 128],
                         inc="v" if cc == 3 else None)
        P.wait("tensor", "v")
        tmark = {}
        a_ev, v_sq = {}, {}
        for fb in range(NCH):
            hf = fb % 2
            h4 = fb % 4
            PSq = ps[0] if hf == 0 else ps[4]
            QTp = QT[:, h4 * 512:(h4 + 1) * 512]
            SQp = SQ[:, h4 * 512:(h4 + 1) * 512]
            QBp = QBF[:, h4 * 512:(h4 + 1) * 512]
            if fb >= 2:
                P.wait("sync", "t", tmark[fb - 2])
            wv_ = P.dma("sync", WF[:, hf * 2048:(hf + 1) * 2048],
                        h["wq"][fb], stream="wq" + str(hf))
            P.wait("tensor", "dS_wq" + str(hf), wv_)
            if fb - 2 in a_ev:
                P.wait("tensor", "a", a_ev[fb - 2])
            else:
                P.wait("tensor", "a")
            for c in range(NCH):
                tmark[fb] = P.op("tensor", "matmul", PSq[:],
                     WF[:, hf * 2048 + c * 128: hf * 2048 + (c + 1) * 128],
                     XF32[:, c * 512:(c + 1) * 512],
                     start=(c == 0), stop=(c == NCH - 1),
                     inc="t" if c == NCH - 1 else None)
            P.wait("scalar", "t", tmark[fb])
            if fb - 4 in v_sq:
                P.wait("scalar", "v", v_sq[fb - 4])
            P.op("scalar", "activation", QTp, PSq[:], AF.Identity,
                 bias=BQ[:, fb:fb + 1], scale=1.0, inc="a")
            a_ev[fb] = P.sems["a"].n
            P.wait("vector", "a", a_ev[fb])
            P.wait("vector", "dG_qo" + str(h4))
            P.op("vector", "tensor_mul", SQp, QTp, QTp)
            v_sq[fb] = P.op("vector", "tensor_copy", QBp, QTp, inc="v")
            P.wait("tensor", "v", v_sq[fb])
            for sub in range(4):
                j = tb * 4 + sub
                col = 2 * (j * 16 + fb)
                P.op("tensor", "matmul", PS_NRM[:, col:col + 1],
                     SQp[:, sub * 128:(sub + 1) * 128], onesf[:],
                     start=True, stop=True, inc="t" if sub == 3 else None)
            P.wait("gpsimd", "v", v_sq[fb])
            P.dma("gpsimd", h["qfm"][fb * 128:(fb + 1) * 128, tb * 512:(tb + 1) * 512],
                  QBp, stream="qo" + str(h4))
            k.pgw_step(1)
    P.wait("vector", "t")
    P.op("vector", "tensor_copy", NORMS2[:],
         PS_NRM[:].rearrange("p (c two) -> p c two", two=2)[:, :, 0:1].squeeze(2),
         inc="v")
    P.wait("gpsimd", "v")
    P.dma("gpsimd", h["norms"], NORMS2[:], stream="no")
    P.barrier()

    for which in ("k", "v"):
        W = h["wk"] if which == "k" else h["wv"]
        Bp = BK if which == "k" else BV
        tmark = {}
        kv_aev = {}
        for fb in range(NCH):
            hf = fb % 2
            if fb >= 2:
                P.wait("gpsimd", "t", tmark[fb - 2])
            wv_ = P.dma("gpsimd", WBF[:, hf * 2048:(hf + 1) * 2048],
                        W[fb], stream="w" + which + str(hf))
            P.wait("tensor", "dG_w" + which + str(hf), wv_)
            for tb in range(4):
                kpar = (fb * 4 + tb) % 2
                PSkv = ps[0] if kpar == 0 else ps[3]
                if fb * 4 + tb >= 2 and (fb * 4 + tb - 2) in kv_aev:
                    P.wait("tensor", "a", kv_aev[fb * 4 + tb - 2])
                else:
                    P.wait("tensor", "a")
                for c in range(NCH):
                    tmark[fb] = P.op("tensor", "matmul", PSkv[:],
                         WBF[:, hf * 2048 + c * 128: hf * 2048 + (c + 1) * 128],
                         XFM[:, c * 2048 + tb * 512: c * 2048 + (tb + 1) * 512],
                         start=(c == 0), stop=(c == NCH - 1),
                         inc="t" if c == NCH - 1 else None)
                P.wait("scalar", "t", tmark[fb])
                if which == "k":
                    P.wait("scalar", "dG_kv")
                    P.op("scalar", "activation", KVBF[:, tb * 512:(tb + 1) * 512],
                         PSkv[:], AF.Identity, bias=Bp[:, fb:fb + 1], scale=1.0,
                         inc="a")
                    kv_aev[fb * 4 + tb] = P.sems["a"].n
                else:
                    P.wait("scalar", "t")
                    P.op("scalar", "activation", VF, PSkv[:], AF.Identity,
                         bias=Bp[:, fb:fb + 1], scale=1.0, inc="a")
                    kv_aev[fb * 4 + tb] = P.sems["a"].n
                    P.wait("tensor", "a")
                    P.wait("tensor", "v")
                    for sub in range(4):
                        P.op("tensor", "transpose", PS_B[:, sub * 128:(sub + 1) * 128],
                             VF[:, sub * 128:(sub + 1) * 128], idb[:],
                             inc="t" if sub == 3 else None)
                    P.wait("vector", "t")
                    P.wait("vector", "dG_kv")
                    for sub in range(4):
                        P.op("vector", "tensor_copy",
                             KVBF[:, sub * 128:(sub + 1) * 128],
                             PS_B[:, sub * 128:(sub + 1) * 128],
                             inc="v" if sub == 3 else None)
                    P.wait("gpsimd", "v")
                    P.dma("gpsimd",
                          h["vtm"].rearrange("(jj p) f -> p jj f", p=128)[
                              :, tb * 4:(tb + 1) * 4, fb * 128:(fb + 1) * 128],
                          KVBF[:, 0:512].rearrange("p (s d) -> p s d", s=4),
                          stream="kv")
            if which == "k":
                P.wait("gpsimd", "a")
                P.dma("gpsimd", h["kfm"][fb * 128:(fb + 1) * 128, :], KVBF[:],
                      stream="kv")
                k.pgw_step(2)
            else:
                k.pgw_step(2)
    P.barrier()




def emit_p23(k: K):
    P, h, s, ps = k.P, k.h, k.s, k.ps
    if k.fused and k.pgw is not None:
        k.pgw.emit_cv8()
    NORMS2 = s["NORMS2"]
    MASK, MASKB, INVM, RANK = s["MASK"], s["MASKB"], s["INVM"], s["RANK"]
    CMP = s["CMP"]
    LOa, LOb, HIa, HIb = s["LOa"], s["LOb"], s["HIa"], s["HIb"]
    MID, MIDX, CNT16, GE, TT16 = s["MID"], s["MIDX"], s["CNT16"], s["GE"], s["TT16"]
    M1, HROW = s["M1"], s["HROW"]
    TOTS, TOTJ, OFFJS, OFFROW, PINCS = s["TOTS"], s["TOTJ"], s["OFFJS"], s["OFFROW"], s["PINCS"]
    idb, idf = s["identb"], s["identf"]
    onesf, onesb, onesrf = s["onesf"], s["onesb"], s["onesrf"]
    TRIU, TRIS, IOTA, BE = s["triu"], s["tris"], s["iota"], s["bemask"]
    QH, KH, VH, QTM = s["QH"], s["KH"], s["VH"], s["QTM"]
    S2, S2T = s["S2"], s["S2T"]
    PR, PT, CTXH, CTXO = s["PR"], s["PT"], s["CTXH"], s["CTXO"]
    MX, NM, RS = s["MX"], s["NM"], s["RS"]
    QGB, GSF, GP, GPT, GCF = s["QGB"], s["GSF"], s["GP"], s["GPT"], s["GCF"]
    MXG, NMG, RSG = s["MXG"], s["NMG"], s["RSG"]
    PS_SC, PS_CT, PS_G, PS_X = ps[0], ps[1], ps[2], ps[3]
    PS_CT2 = ps[3]
    PS_B0, PS_B1 = ps[6], ps[7]

    P.op("vector", "memset", onesf[:], 1.0)
    P.op("vector", "memset", onesb[:], 1.0)
    P.op("vector", "memset", onesrf[:], 1.0)
    P.op("vector", "memset", LOa[:], -1.0, inc="v")
    for cn in ("identb", "identf", "triu", "tris", "iota", "bemask"):
        P.dma("sync", s[cn][:], h[cn], stream="c")
    cn_ = P.dma("sync", NORMS2[:], h["norms"], stream="c")
    P.wait("tensor", "dS_c")
    P.wait("vector", "dS_c")
    P.wait("scalar", "dS_c")

    P.op("vector", "tensor_reduce", M1[:],
         NORMS2[:].rearrange("p (j h) -> p h j", h=16),
         AX.X, ALU.max, inc="v")
    P.wait("tensor", "v")
    P.op("tensor", "matmul", PS_G[0:16, 0:128], M1[:], idf[:], start=True,
         stop=True, inc="t")
    P.wait("vector", "t")
    P.op("vector", "tensor_reduce", s["H16C"][:], PS_G[0:16, 0:128], AX.X, ALU.max,
         inc="v")
    P.wait("tensor", "v")
    P.op("tensor", "matmul", PS_X[0:1, 0:16], s["H16C"][:], idf[0:16, 0:16],
         start=True, stop=True, inc="t")
    P.wait("vector", "t")
    P.op("vector", "tensor_copy", HIa[:], PS_X[0:1, 0:16], inc="v")

    for it in range(NBIS):
        LO = LOa if it % 2 == 0 else LOb
        HI = HIa if it % 2 == 0 else HIb
        LO2 = LOb if it % 2 == 0 else LOa
        HI2 = HIb if it % 2 == 0 else HIa
        P.wait("vector", "v")
        P.op("vector", "tensor_add", TT16[:], LO[:], HI[:], inc="v")
        P.wait("vector", "v")
        P.op("vector", "tensor_scalar_mul", MID[:], TT16[:], 0.5, inc="v")
        P.wait("vector", "v")
        P.op("vector", "tensor_copy",
             MIDX[:].rearrange("a (j h) -> a j h", h=16),
             MID[:].unsqueeze(1).broadcast_to([1, 16, 16]), inc="v")
        P.wait("tensor", "v")
        P.op("tensor", "matmul", PS_G[:, 0:256], onesrf[:], MIDX[:],
             start=True, stop=True, inc="t")
        P.wait("vector", "t")
        P.op("vector", "tensor_tensor", CMP[:], NORMS2[:], PS_G[:, 0:256],
             ALU.is_gt, inc="v")
        P.wait("tensor", "v")
        P.op("tensor", "matmul", PS_X[0:1, 0:256], onesb[:], CMP[:],
             start=True, stop=True, inc="t")
        P.wait("vector", "t")
        P.op("vector", "tensor_reduce", CNT16[:],
             PS_X[0:1, 0:256].rearrange("a (j h) -> a h j", h=16),
             AX.X, ALU.add, inc="v")
        P.wait("vector", "v")
        P.op("vector", "tensor_single_scalar", GE[:], CNT16[:], 62.0, ALU.is_ge,
             inc="v")
        P.wait("vector", "v")
        P.op("vector", "select", LO2[:], GE[:], MID[:], LO[:], True, inc="v")
        P.wait("vector", "v")
        P.op("vector", "select", HI2[:], GE[:], HI[:], MID[:], True, inc="v")
        P.wait("vector", "v")
    LOF = LOa if NBIS % 2 == 0 else LOb

    P.op("vector", "tensor_copy",
         MIDX[:].rearrange("a (j h) -> a j h", h=16),
         LOF[:].unsqueeze(1).broadcast_to([1, 16, 16]), inc="v")
    P.wait("tensor", "v")
    P.op("tensor", "matmul", PS_G[:, 0:256], onesrf[:], MIDX[:],
         start=True, stop=True, inc="t")
    P.wait("vector", "t")
    P.op("vector", "tensor_tensor", MASK[:], NORMS2[:], PS_G[:, 0:256], ALU.is_gt,
         inc="v")
    P.wait("vector", "v")
    P.op("vector", "tensor_max", MASK[:], MASK[:], BE[:], inc="v")
    P.wait("vector", "v")
    P.op("vector", "tensor_copy", MASKB[:], MASK[:])
    P.op("vector", "tensor_scalar", INVM[:], MASK[:], -1.0, 1.0, ALU.mult, ALU.add,
         inc="v")
    P.wait("tensor", "v")
    P.op("tensor", "matmul", PS_CT[:, 0:256], TRIU[:], MASKB[:],
         start=True, stop=True)
    P.op("tensor", "matmul", PS_X[0:1, 0:256], onesb[:], MASKB[:],
         start=True, stop=True, inc="t")
    P.wait("vector", "t")
    P.op("vector", "tensor_copy", TOTS[:], PS_X[0:1, 0:256])
    P.op("vector", "tensor_copy", PINCS[:], PS_CT[:, 0:256], inc="v")
    P.wait("gpsimd", "v")
    P.dma("gpsimd", h["sc256a"], TOTS[:], stream="r")
    P.wait("gpsimd", "dG_r")
    P.dma("gpsimd", TOTJ[:], h["sc256a"].rearrange("a (j h) -> j (a h)", h=16),
          stream="r")
    P.wait("tensor", "dG_r")
    P.op("tensor", "matmul", PS_X[0:16, 256:272], TRIS[:], TOTJ[:],
         start=True, stop=True, inc="t")
    P.wait("vector", "t")
    P.op("vector", "tensor_copy", OFFJS[:], PS_X[0:16, 256:272], inc="v")
    P.wait("gpsimd", "v")
    P.dma("gpsimd", h["sc256b"], OFFJS[:], stream="r")
    P.wait("gpsimd", "dG_r")
    P.dma("gpsimd", OFFROW[:], h["sc256b"].rearrange("j h -> (j h)").unsqueeze(0), stream="r")
    P.wait("tensor", "dG_r")
    P.op("tensor", "matmul", PS_G[:, 0:256], onesrf[:], OFFROW[:],
         start=True, stop=True, inc="t")
    P.wait("vector", "t")
    P.op("vector", "scalar_tensor_tensor", RANK[:], PINCS[:], -1.0, PS_G[:, 0:256],
         ALU.add, ALU.add, inc="v")
    P.barrier()

    s1m = {}
    s2m = {}

    def stage1(fb):
        par = fb % 2
        QHh = QH[:, par * 2048:(par + 1) * 2048]
        KHh = KH[:, par * 2048:(par + 1) * 2048]
        VHh = VH[:, par * 2048:(par + 1) * 2048]
        QTMh = QTM[:, par * 2048:(par + 1) * 2048]
        S2h = S2[:, par * NT * SLACK:(par + 1) * NT * SLACK]
        S2Th = S2T[:, par * 2048:(par + 1) * 2048]
        CTXHh = CTXH[:, par * 2048:(par + 1) * 2048]
        if fb - 2 in s2m:
            P.wait("sync", "t", s2m[fb - 2]["t"])
            P.wait("sync", "v", s2m[fb - 2]["v"])
            P.wait("gpsimd", "t", s2m[fb - 2]["t"])
        else:
            P.wait("sync", "t")
            P.wait("gpsimd", "t")
        wq_ = P.dma("sync", QHh, h["qfm"][fb * 128:(fb + 1) * 128, :],
                    stream="qh" + str(par))
        wk_ = P.dma("sync", KHh, h["kfm"][fb * 128:(fb + 1) * 128, :],
                    stream="qh" + str(par))
        wv_ = P.dma("gpsimd", VHh.rearrange("p (j d) -> p j d", d=128),
                    h["vtm"].rearrange("(j p) f -> p j f", p=128)[
                        :, :, fb * 128:(fb + 1) * 128], stream="vh" + str(par))
        P.wait("tensor", "dS_qh" + str(par), wk_)
        for half in range(2):
            P.wait("tensor", "a")
            for cc in range(8):
                j = half * 8 + cc
                P.op("tensor", "transpose", PS_B0[:, cc * 128:(cc + 1) * 128],
                     QHh[:, j * 128:(j + 1) * 128], idb[:],
                     inc="t" if cc == 7 else None)
            P.wait("scalar", "t")
            for cc in range(8):
                j = half * 8 + cc
                P.op("scalar", "copy", QTMh[:, j * 128:(j + 1) * 128],
                     PS_B0[:, cc * 128:(cc + 1) * 128],
                     inc="a" if cc == 7 else None)
        P.wait("vector", "t")
        P.wait("vector", "v")
        for j in range(NT):
            col = j * 16 + fb
            P.op("vector", "tensor_scalar", S2h[:, j * SLACK:(j + 1) * SLACK],
                 IOTA[:], RANK[:, col:col + 1], MASK[:, col:col + 1],
                 ALU.is_equal, ALU.mult, inc="v" if j == NT - 1 else None)
        P.wait("tensor", "v")
        for half in range(2):
            P.wait("tensor", "a")
            for cc in range(8):
                j = half * 8 + cc
                P.op("tensor", "transpose", PS_B1[0:72, cc * 128:cc * 128 + 128],
                     S2h[:, j * SLACK:(j + 1) * SLACK], idb[:],
                     inc="t" if cc == 7 else None)
            P.wait("scalar", "t")
            for cc in range(8):
                j = half * 8 + cc
                P.op("scalar", "copy", S2Th[0:72, j * 128:(j + 1) * 128],
                     PS_B1[0:72, cc * 128:cc * 128 + 128],
                     inc="a" if cc == 7 else None)
        P.wait("tensor", "dS_qh" + str(par), wk_)
        P.wait("tensor", "dG_vh" + str(par), wv_)
        P.wait("vector", "t")
        P.op("vector", "memset", PR[:], 0.0, inc="v")
        t_sc, t_tr, t_pv = {}, {}, {}
        a_exp, a_ctx = {}, {}
        v_max, v_pt, v_rs = {}, {}, {}
        for qb in range(NT):
            qpar = qb % 2
            SB = ps[0] if qpar == 0 else ps[4]
            CTb = ps[1] if qpar == 0 else ps[5]
            PRh = PR[:, qpar * 384:(qpar + 1) * 384]
            PTh = PT[:, qpar * 384:(qpar + 1) * 384]
            cl0 = max(qb - 1, 0)
            cl1 = min(qb + 2, NT)
            W = (cl1 - cl0) * 128
            v0 = max(qb * 128 - 64 - cl0 * 128, 0)
            v1 = min(qb * 128 + 192 - cl0 * 128, W)
            nch = cl1 - cl0
            if qb - 2 in a_exp:
                P.wait("tensor", "a", a_exp[qb - 2])
                P.wait("tensor", "v", v_max[qb - 2])
            else:
                P.wait("tensor", "v")
            for i, ch in enumerate(range(cl0, cl1)):
                t_sc[qb] = P.op("tensor", "matmul", SB[:, i * 128:(i + 1) * 128],
                     QHh[:, qb * 128:(qb + 1) * 128],
                     KHh[:, ch * 128:(ch + 1) * 128],
                     start=True, stop=True, inc="t" if ch == cl1 - 1 else None)
            P.wait("vector", "t", t_sc[qb])
            v_max[qb] = P.op("vector", "reduce_max", MX[:, qpar:qpar + 1],
                             SB[:, v0:v1], axis=AX.X, inc="v")
            P.wait("vector", "v", v_max[qb])
            P.op("vector", "tensor_scalar_mul", NM[:, qpar:qpar + 1],
                 MX[:, qpar:qpar + 1], -SCALE, inc="v")
            if qb == 2:
                P.wait("vector", "t", t_tr[0])
                P.op("vector", "memset", PR[:, 0:64], 0.0, inc="v")
            P.wait("scalar", "v")
            P.op("scalar", "activation", PRh[:, v0:v1], SB[:, v0:v1], AF.Exp,
                 bias=NM[:, qpar:qpar + 1], scale=SCALE,
                 accum_out=s["SUMS"][:, qpar:qpar + 1], inc="a")
            a_exp[qb] = P.sems["a"].n
            P.wait("vector", "a", a_exp[qb])
            v_rs[qb] = P.op("vector", "reciprocal", RS[:, qpar:qpar + 1],
                            s["SUMS"][:, qpar:qpar + 1], inc="v")
            P.wait("tensor", "a", a_exp[qb])
            if qb - 2 in v_pt:
                P.wait("tensor", "a", v_pt[qb - 2])
            for i in range(nch):
                t_tr[qb] = P.op("tensor", "transpose",
                     PS_B0[:, qpar * 384 + i * 128: qpar * 384 + (i + 1) * 128],
                     PRh[:, i * 128:(i + 1) * 128], idb[:],
                     inc="t" if i == nch - 1 else None)
            P.wait("scalar", "t", t_tr[qb])
            if qb - 2 in t_pv:
                P.wait("scalar", "t", t_pv[qb - 2])
            for i in range(nch):
                P.op("scalar", "copy",
                     PTh[:, i * 128:(i + 1) * 128],
                     PS_B0[:, qpar * 384 + i * 128: qpar * 384 + (i + 1) * 128],
                     inc="a" if i == nch - 1 else None)
            v_pt[qb] = P.sems["a"].n
            P.wait("tensor", "a", v_pt[qb])
            if qb - 2 in a_ctx:
                P.wait("tensor", "a", a_ctx[qb - 2])
            for i, ch in enumerate(range(cl0, cl1)):
                t_pv[qb] = P.op("tensor", "matmul", CTb[:, 0:128],
                     PTh[:, i * 128:(i + 1) * 128],
                     VHh[:, ch * 128:(ch + 1) * 128],
                     start=(i == 0), stop=(i == nch - 1),
                     inc="t" if i == nch - 1 else None)
            P.wait("scalar", "t", t_pv[qb])
            P.wait("scalar", "v", v_rs[qb])
            P.op("scalar", "activation", CTXHh[:, qb * 128:(qb + 1) * 128],
                 CTb[:, 0:128], AF.Identity, scale=RS[:, qpar:qpar + 1],
                 inc="a")
            a_ctx[qb] = P.sems["a"].n
            if qb in (4, 9, 14):
                k.pgw_step(1)
        s1m[fb] = {"t": P.sems["t"].n, "v": P.sems["v"].n, "a": P.sems["a"].n}

    def stage2(fb):
        par = fb % 2
        QHh = QH[:, par * 2048:(par + 1) * 2048]
        KHh = KH[:, par * 2048:(par + 1) * 2048]
        VHh = VH[:, par * 2048:(par + 1) * 2048]
        QTMh = QTM[:, par * 2048:(par + 1) * 2048]
        S2h = S2[:, par * NT * SLACK:(par + 1) * NT * SLACK]
        S2Th = S2T[:, par * 2048:(par + 1) * 2048]
        CTXHh = CTXH[:, par * 2048:(par + 1) * 2048]
        CTXOh = CTXO[:, par * 2048:(par + 1) * 2048]
        P.wait("tensor", "v", s1m[fb]["v"])
        P.wait("tensor", "a", s1m[fb]["a"])
        for j in range(NT):
            P.op("tensor", "matmul", PS_G[:, 0:SLACK],
                 QTMh[:, j * 128:(j + 1) * 128],
                 S2h[:, j * SLACK:(j + 1) * SLACK],
                 start=(j == 0), stop=(j == NT - 1),
                 inc="t" if j == NT - 1 else None)
        P.wait("vector", "t")
        P.op("vector", "tensor_copy", QGB[:], PS_G[:, 0:SLACK], inc="v")
        P.wait("tensor", "v")
        for tb in range(4):
            P.op("tensor", "matmul", PS_X[0:SLACK, 0:512], QGB[:],
                 KHh[:, tb * 512:(tb + 1) * 512], start=True, stop=True,
                 inc="t")
            P.wait("scalar", "t")
            P.op("scalar", "copy", GSF[:, tb * 512:(tb + 1) * 512],
                 PS_X[0:SLACK, 0:512], inc="a")
            P.wait("tensor", "a")
        P.wait("vector", "a")
        P.op("vector", "reduce_max", MXG[:], GSF[:], axis=AX.X, inc="v")
        P.wait("vector", "v")
        P.op("vector", "tensor_scalar_mul", NMG[:], MXG[:], -SCALE, inc="v")
        P.wait("scalar", "v")
        P.op("scalar", "activation", GP[:], GSF[:], AF.Exp, bias=NMG[:, 0:1],
             scale=SCALE, accum_out=s["GSUM"][:], inc="a")
        P.wait("vector", "a")
        P.op("vector", "reciprocal", RSG[:], s["GSUM"][:], inc="v")
        k.pgw_step(1)
        P.wait("tensor", "a")
        for half in range(2):
            P.wait("tensor", "a")
            for cc in range(8):
                j = half * 8 + cc
                P.op("tensor", "transpose", PS_B1[:, cc * 128:cc * 128 + 72],
                     GP[0:72, j * 128:(j + 1) * 128], idb[0:72, 0:72],
                     inc="t" if cc == 7 else None)
            P.wait("scalar", "t")
            for cc in range(8):
                j = half * 8 + cc
                P.op("scalar", "copy", GPT[:, j * SLACK:(j + 1) * SLACK],
                     PS_B1[:, cc * 128:cc * 128 + 72],
                     inc="a" if cc == 7 else None)
        P.wait("tensor", "v")
        P.wait("tensor", "a")
        for j in range(NT):
            P.op("tensor", "matmul", PS_G[0:SLACK, 0:128],
                 GPT[:, j * SLACK:(j + 1) * SLACK],
                 VHh[:, j * 128:(j + 1) * 128],
                 start=(j == 0), stop=(j == NT - 1),
                 inc="t" if j == NT - 1 else None)
        P.wait("vector", "t")
        P.op("vector", "tensor_scalar_mul", GCF[:], PS_G[0:SLACK, 0:128],
             RSG[:, 0:1], inc="v")
        P.wait("tensor", "v")
        P.wait("tensor", "a")
        P.wait("vector", "a")
        P.wait("vector", "dG_co" + str(par))
        for j in range(NT):
            col = j * 16 + fb
            P.op("tensor", "matmul", PS_CT2[:, 0:128],
                 S2Th[0:72, j * 128:(j + 1) * 128], GCF[:],
                 start=True, stop=True, inc="t")
            P.wait("vector", "t")
            P.op("vector", "scalar_tensor_tensor", CTXOh[:, j * 128:(j + 1) * 128],
                 CTXHh[:, j * 128:(j + 1) * 128], INVM[:, col:col + 1],
                 PS_CT2[:, 0:128], ALU.mult, ALU.add, inc="v")
            P.wait("tensor", "v")
        P.wait("gpsimd", "v")
        P.wait("gpsimd", "dG_co" + str(par))
        P.dma("gpsimd",
              h["ctx"].rearrange("(j p) f -> p j f", p=128)[:, :, fb * 128:(fb + 1) * 128],
              CTXOh.rearrange("p (j d) -> p j d", d=128), stream="co" + str(par))
        k.pgw_step(1 if fb < 13 else 4)
        s2m[fb] = {"t": P.sems["t"].n, "v": P.sems["v"].n, "a": P.sems["a"].n}

    for fb in range(NH + 1):
        if fb < NH:
            stage1(fb)
        if fb >= 1:
            stage2(fb - 1)
    if k.pgw is not None:
        k.pgw.flush()
    P.barrier()



def emit_ln(k: K, a_name, b_name, out_name, build_fm=False):
    P, h, s = k.P, k.h, k.s
    XT4, CT4, SQ4, X1T = s["XT4"], s["CT4"], s["SQ4"], s["X1T"]
    SM, VARS, RSTD = s["SM"], s["VARS"], s["RSTD"]
    if build_fm:
        RFM, idb, PS_B = s["RFM"], s["identb"], k.ps[6]
        P.dma("sync", idb[:], h["identb"], stream="c")
        P.wait("tensor", "dS_c")
    P.op("vector", "memset", s["EPS4"][:], 1e-5, inc="v")
    P.wait("scalar", "v")
    mk = {}
    for j in range(NT):
        par = j % 2
        XTp = XT4[:, par * 2048:(par + 1) * 2048]
        CTp = CT4[:, par * 2048:(par + 1) * 2048]
        X1p = X1T[:, par * 2048:(par + 1) * 2048]
        SQp = SQ4[:, par * 2048:(par + 1) * 2048]
        SMc = SM[:, par:par + 1]
        VRc = VARS[:, par:par + 1]
        RSc = RSTD[:, par:par + 1]
        if j >= 2:
            P.wait("sync", "v", mk[("x1", j - 2)])
            P.wait("sync", "a", mk[("sq", j - 2)])
        wa = P.dma("sync", XTp, h[a_name][j * 128:(j + 1) * 128, :],
                   stream="la" + str(par))
        wb = P.dma("sync", CTp, h[b_name][j * 128:(j + 1) * 128, :],
                   stream="lb" + str(par))
        P.wait("vector", "dS_la" + str(par), wa)
        P.wait("vector", "dS_lb" + str(par), wb)
        if j >= 2 and build_fm:
            P.wait("vector", "a", mk[("sq", j - 2)])
        P.op("vector", "tensor_add", XTp, XTp, CTp, inc="v")
        P.wait("vector", "v")
        P.op("vector", "reduce_sum", SMc, XTp, axis=AX.X, inc="v")
        P.wait("vector", "v")
        P.op("vector", "tensor_scalar_mul", SMc, SMc, 1.0 / DM, inc="v")
        P.wait("vector", "v")
        mk[("ctr", j)] = P.op("vector", "tensor_scalar", XTp, XTp, SMc, None,
                              ALU.subtract, inc="v")
        P.wait("scalar", "v", mk[("ctr", j)])
        P.wait("scalar", "a")
        if j >= 2 and build_fm:
            P.wait("scalar", "t", mk[("tr", j - 2)])
        P.op("scalar", "activation", SQp, XTp, AF.Square,
             accum_out=VRc, inc="a")
        mk[("sq", j)] = P.sems["a"].n
        P.wait("scalar", "a")
        P.op("scalar", "activation", VRc, VRc, AF.Ln,
             bias=s["EPS4"][:, 0:1], scale=1.0 / DM, inc="a")
        P.wait("vector", "a")
        P.op("vector", "tensor_scalar_mul", VRc, VRc, -0.5, inc="v")
        P.wait("scalar", "v")
        P.op("scalar", "activation", RSc, VRc, AF.Exp, inc="a")
        mk[("rstd", j)] = P.sems["a"].n
        P.wait("vector", "a", mk[("rstd", j)])
        P.wait("vector", "dG_lo" + str(par))
        mk[("x1", j)] = P.op("vector", "tensor_scalar_mul", X1p, XTp,
                             RSc, inc="v")
        P.wait("gpsimd", "v", mk[("x1", j)])
        P.dma("gpsimd", h[out_name][j * 128:(j + 1) * 128, :], X1p,
              stream="lo" + str(par))
        if out_name == "x1":
            k.pgw_step(2)
        if build_fm:
            XB = SQp.bitcast(BF16)[:, 0:2048]
            P.wait("vector", "a", mk[("sq", j)])
            P.wait("vector", "v", mk[("x1", j)])
            mk[("xb", j)] = P.op("vector", "tensor_copy", XB, X1p, inc="v")
            P.wait("tensor", "v", mk[("xb", j)])
            for half in range(2):
                P.wait("tensor", "v")
                for cc in range(8):
                    c = half * 8 + cc
                    mk[("tr", j)] = P.op("tensor", "transpose",
                         PS_B[:, cc * 128:(cc + 1) * 128],
                         XB[:, c * 128:(c + 1) * 128], idb[:],
                         inc="t" if cc == 7 else None)
                P.wait("vector", "t")
                for cc in range(8):
                    c = half * 8 + cc
                    P.op("vector", "tensor_copy",
                         RFM[:, c * 2048 + j * 128: c * 2048 + (j + 1) * 128],
                         PS_B[:, cc * 128:(cc + 1) * 128],
                         inc="v" if cc == 7 else None)
    P.barrier()


def emit_ln1(k): emit_ln(k, "x", "ctx", "x1", build_fm=k.fused)
def emit_ln2(k): emit_ln(k, "x1", "pre2", "x2", build_fm=k.fused)
def emit_ln3(k): emit_ln(k, "x2", "pre3", "x3", build_fm=k.fused)
def emit_ln4(k): emit_ln(k, "x3", "pre4", "y")


def emit_fm(k: K, src_name, xb4=None):
    P, h, s, ps = k.P, k.h, k.s, k.ps
    RFM, XT4 = s["RFM"], s["XT4"]
    XB4 = xb4 if xb4 is not None else s["XB4"]
    idb = s["identb"]
    PS_B = ps[6]
    P.dma("sync", idb[:], h["identb"], stream="c")
    P.wait("tensor", "dS_c")
    for j in range(NT):
        P.wait("sync", "v")
        P.dma("sync", XT4[:, 0:2048], h[src_name][j * 128:(j + 1) * 128, :],
              stream="fx")
        P.wait("vector", "dS_fx")
        P.wait("vector", "t")
        P.op("vector", "tensor_copy", XB4, XT4[:, 0:2048], inc="v")
        P.wait("tensor", "v")
        for half in range(2):
            P.wait("tensor", "v")
            for cc in range(8):
                c = half * 8 + cc
                P.op("tensor", "transpose", PS_B[:, cc * 128:(cc + 1) * 128],
                     XB4[:, c * 128:(c + 1) * 128], idb[:],
                     inc="t" if cc == 7 else None)
            P.wait("vector", "t")
            for cc in range(8):
                c = half * 8 + cc
                P.op("vector", "tensor_copy",
                     RFM[:, c * 2048 + j * 128: c * 2048 + (j + 1) * 128],
                     PS_B[:, cc * 128:(cc + 1) * 128],
                     inc="v" if cc == 7 else None)
    P.barrier()


def emit_pgw(k: K):
    if k.pgw is None:
        k.pgw = Pgw(k)
    if not k.pgw.done_prolog:
        k.pgw.prolog()
    if not k.fused:
        k.pgw.flush()
    k.P.barrier()


def emit_p5(k: K):
    if not k.fused:
        emit_fm(k, "x1", k.s["H5"][:, 0:2048])
    P, h, s, ps = k.P, k.h, k.s, k.ps
    RFM, H5 = s["RFM"], s["H5"]
    WGB5, WUB5, WD5 = s["WGB5"], s["WUB5"], s["WD5"]
    SG5, P5E = s["SG5"], s["P5E"]
    PS_G5, PS_U5, PS_D0, PS_D1 = ps[0], ps[1], ps[2], ps[3]
    for th in range(2):
        tmark = {}
        for ot in range(NI0):
            hf = ot % 2
            if ot >= 2:
                P.wait("sync", "t", tmark[ot - 2])
            P.dma("sync", WGB5[:, hf * 2048:(hf + 1) * 2048], h["wgb"][ot],
                  stream="wg5" + str(hf))
            wv_ = P.dma("sync", WUB5[:, hf * 2048:(hf + 1) * 2048], h["wub"][ot],
                        stream="wg5" + str(hf))
            P.wait("tensor", "dS_wg5" + str(hf), wv_)
            for tb2 in range(2):
                tc = th * 1024 + tb2 * 512
                par = tb2 % 2
                PS_G5 = k.ps[0] if par == 0 else k.ps[2]
                PS_U5 = k.ps[1] if par == 0 else k.ps[3]
                if ot == 0:
                    P.wait("tensor", "a")
                    P.wait("tensor", "v")
                else:
                    P.wait("tensor", "a", max(P.sems["a"].n - 1, 0) if "a" in P.sems else 0)
                    P.wait("tensor", "v", max(P.sems["v"].n - 1, 0) if "v" in P.sems else 0)
                for c in range(NCH):
                    P.op("tensor", "matmul", PS_G5[:],
                         WGB5[:, hf * 2048 + c * 128: hf * 2048 + (c + 1) * 128],
                         RFM[:, c * 2048 + tc: c * 2048 + tc + 512],
                         start=(c == 0), stop=(c == NCH - 1))
                for c in range(NCH):
                    tmark[ot] = P.op("tensor", "matmul", PS_U5[:],
                         WUB5[:, hf * 2048 + c * 128: hf * 2048 + (c + 1) * 128],
                         RFM[:, c * 2048 + tc: c * 2048 + tc + 512],
                         start=(c == 0), stop=(c == NCH - 1),
                         inc="t" if c == NCH - 1 else None)
                P.wait("scalar", "t")
                P.wait("scalar", "v")
                P.op("scalar", "activation", SG5[:], PS_G5[:], AF.Silu, inc="a")
                P.wait("vector", "a")
                P.op("vector", "tensor_mul",
                     H5[:, ot * 1024 + tb2 * 512: ot * 1024 + tb2 * 512 + 512],
                     SG5[:], PS_U5[:], inc="v")
        evmk = {}
        for fg in range(4):
            tprev = None
            for ic in range(NI0):
                hf = ic % 2
                if tprev is not None:
                    P.wait("sync", "t", tprev)
                else:
                    P.wait("sync", "t")
                wd_ = P.dma("sync", WD5[:, hf * 512:(hf + 1) * 512],
                            h["wdb"][ic][:, fg * 512:(fg + 1) * 512],
                            stream="wd5" + str(hf))
                P.wait("tensor", "dS_wd5" + str(hf), wd_)
                for t8 in range(8):
                    if ic == 0 and (fg - 1, t8) in evmk:
                        P.wait("tensor", "v", evmk[(fg - 1, t8)])
                    elif ic == 0 and t8 == 0:
                        P.wait("tensor", "v")
                    outap = k.ps[t8][:] if t8 < 6 else k.ps[t8][:].bitcast(F32)
                    tprev = P.op("tensor", "matmul", outap,
                         H5[:, ic * 1024 + t8 * 128: ic * 1024 + (t8 + 1) * 128],
                         WD5[:, hf * 512:(hf + 1) * 512],
                         start=(ic == 0), stop=(ic == NI0 - 1),
                         inc="t" if t8 == 7 else None)
            P.wait("vector", "t")
            P.wait("vector", "dG_p5")
            for t8 in range(8):
                inap = k.ps[t8][:] if t8 < 6 else k.ps[t8][:].bitcast(F32)
                evmk[(fg, t8)] = P.op("vector", "tensor_copy",
                     P5E[:, t8 * 512:(t8 + 1) * 512],
                     inap, inc="v")
            P.wait("gpsimd", "v")
            P.dma("gpsimd",
                  h["pre2"].rearrange("(tt p) f -> p tt f", p=128)[
                      :, th * 8:(th + 1) * 8, fg * 512:(fg + 1) * 512],
                  P5E[:].rearrange("p (tt d) -> p tt d", d=512), stream="p5")
    P.barrier()


def emit_p7(k: K):
    if not k.fused:
        emit_fm(k, "x2", k.s["XFM"][:, 0:2048])
    if k.fused and k.pgw is not None and k.pgw.cv8 is not None:
        k.P.wait("sync", "dG_cv8", k.pgw.cv8)
    P, h, s, ps = k.P, k.h, k.s, k.ps
    RFM, AFM = s["RFM"], s["XFM"]
    WB7 = s["WB7"]
    QPH, KPH, VPH, RKV, RKS = s["QPH"], s["KPH"], s["VPH"], s["RKV"], s["RKS"]
    TMP7, E7 = s["TMP7"], s["E7"]
    DEC1, DECB = s["DEC1"], s["DECB"]
    DEN7 = s["DEN7"]
    OUT7 = s["XT4"]
    onesf, onesrf = s["onesf"], s["onesrf"]
    PS_A, PS_Bc, PS_R = ps[0], ps[1], ps[2]

    P.op("vector", "memset", onesf[:], 1.0)
    P.op("vector", "memset", s["onesb"][:], 1.0)
    P.op("vector", "memset", onesrf[:], 1.0, inc="v")
    cd = P.dma("sync", DEC1[:], h["decay"], stream="c")
    P.wait("scalar", "dS_c", cd)
    P.op("scalar", "activation", DEC1[:], DEC1[:], AF.Sigmoid, inc="a")
    P.wait("tensor", "a")
    P.wait("tensor", "v")
    P.op("tensor", "matmul", PS_R[:, 0:16], onesrf[:], DEC1[:],
         start=True, stop=True, inc="t")
    P.wait("vector", "t")
    P.op("vector", "tensor_copy", DECB[:], PS_R[:, 0:16], inc="v")

    tmark = {}
    nsl = 0
    q_done = {}
    s2v = {}

    def stage1(fb):
        nonlocal nsl
        par = fb % 2
        QPHh = QPH[:, par * 2048:(par + 1) * 2048]
        KPHh = KPH[:, par * 2048:(par + 1) * 2048]
        VPHh = VPH[:, par * 2048:(par + 1) * 2048]
        if fb - 2 in s2v:
            P.wait("vector", "v", s2v[fb - 2])
        for slot, (dst, phi) in enumerate(
                ((QPHh, True), (KPHh, True), (VPHh, False))):
            hf = nsl % 2
            if nsl >= 2:
                P.wait("sync" if k.fused else "gpsimd", "t", tmark[nsl - 2])
            if k.fused:
                wv_ = P.dma("sync", WB7[:, hf * 2048:(hf + 1) * 2048],
                            h["wqkv8"][slot, fb], stream="w7" + str(hf))
                P.wait("tensor", "dS_w7" + str(hf), wv_)
            else:
                wv_ = P.dma("gpsimd", WB7[:, hf * 2048:(hf + 1) * 2048],
                            h["wqkv"][slot, fb], stream="w7" + str(hf))
                P.wait("tensor", "dG_w7" + str(hf), wv_)
            for tb in range(4):
                tpar = tb % 2
                PS_Ax = ps[0] if tpar == 0 else ps[4]
                if nsl == 0 and tb < 2:
                    P.wait("tensor", "v")
                else:
                    P.wait("tensor", "v", max(P.sems["v"].n - 1, 0))
                for c in range(NCH):
                    tmark[nsl] = P.op("tensor", "matmul", PS_Ax[:],
                         WB7[:, hf * 2048 + c * 128: hf * 2048 + (c + 1) * 128],
                         RFM[:, c * 2048 + tb * 512: c * 2048 + (tb + 1) * 512],
                         start=(c == 0), stop=(c == NCH - 1),
                         inc="t" if c == NCH - 1 else None)
                dslice = dst[:, tb * 512:(tb + 1) * 512]
                if phi:
                    P.wait("vector", "t")
                    P.op("vector", "tensor_scalar_min", TMP7[:], PS_Ax[:], 0.0,
                         inc="vb")
                    P.wait("scalar", "vb")
                    P.op("scalar", "activation", E7[:], TMP7[:], AF.Exp, inc="a")
                    P.wait("vector", "a")
                    P.op("vector", "scalar_tensor_tensor", dslice, PS_Ax[:], 0.0,
                         E7[:], ALU.max, ALU.add, inc="v")
                else:
                    P.wait("vector", "t")
                    P.op("vector", "tensor_copy", dslice, PS_Ax[:], inc="v")
            nsl += 1
        q_done[fb] = P.sems["v"].n

    def stage2(fb):
        par = fb % 2
        QPHh = QPH[:, par * 2048:(par + 1) * 2048]
        KPHh = KPH[:, par * 2048:(par + 1) * 2048]
        VPHh = VPH[:, par * 2048:(par + 1) * 2048]
        RKVh = RKV[:, par * 2048:(par + 1) * 2048]
        RKSh = RKS[:, par * 2048:(par + 1) * 2048]
        P.wait("vector", "v", q_done[fb])
        P.op("vector", "tensor_mul", VPHh, KPHh, VPHh, inc="v")
        P.wait("vector", "v")
        P.op("vector", "tensor_tensor_scan", RKVh,
             DECB[:, fb:fb + 1].broadcast_to([128, T]), VPHh, 0.0,
             ALU.mult, ALU.add, inc="v")
        P.wait("vector", "v")
        P.op("vector", "tensor_tensor_scan", RKSh,
             DECB[:, fb:fb + 1].broadcast_to([128, T]), KPHh, 0.0,
             ALU.mult, ALU.add, inc="v")
        P.wait("vector", "v")
        P.op("vector", "tensor_mul", VPHh, QPHh, RKSh, inc="v")
        vpp = P.sems["v"].n
        for tb in range(4):
            P.wait("tensor", "v", vpp)
            if tb > 0:
                P.wait("tensor", "v")
            P.op("tensor", "matmul", PS_R[0:1, 0:512], s["onesb"][:],
                 VPHh[:, tb * 512:(tb + 1) * 512], start=True, stop=True,
                 inc="t")
            P.wait("vector", "t")
            P.op("vector", "tensor_copy", DEN7[:, tb * 512:(tb + 1) * 512],
                 PS_R[0:1, 0:512], inc="v")
        P.wait("vector", "v")
        P.op("vector", "tensor_scalar_max", DEN7[:], DEN7[:], 1e-6, inc="v")
        P.wait("vector", "v")
        P.op("vector", "reciprocal", DEN7[:], DEN7[:], inc="v")
        vden = P.sems["v"].n
        P.wait("vector", "v")
        P.op("vector", "tensor_mul", RKVh, QPHh, RKVh, inc="v")
        vnum = P.sems["v"].n
        for tb in range(4):
            P.wait("tensor", "v", vden)
            if tb > 0:
                P.wait("tensor", "v")
            P.op("tensor", "matmul", PS_Bc[:], onesrf[:],
                 DEN7[:, tb * 512:(tb + 1) * 512], start=True, stop=True,
                 inc="t")
            P.wait("vector", "t")
            P.wait("vector", "v", vnum)
            P.op("vector", "tensor_mul",
                 AFM[:, fb * 2048 + tb * 512: fb * 2048 + (tb + 1) * 512],
                 RKVh[:, tb * 512:(tb + 1) * 512], PS_Bc[:], inc="v")
            P.wait("tensor", "v")
        s2v[fb] = P.sems["v"].n

    for fb in range(NH + 1):
        if fb < NH:
            stage1(fb)
        if fb >= 1:
            stage2(fb - 1)
    if k.pgw is not None:
        k.pgw.flush()
    P.barrier()
    WOB = s["WB7"]
    for jh in range(2):
        for fo in range(4):
            tprev = None
            for c in range(NCH):
                wslice = WOB[:, (c % 2) * 512:(c % 2) * 512 + 512]
                if tprev is not None:
                    P.wait("sync" if k.fused else "gpsimd", "t", tprev)
                else:
                    P.wait("sync" if k.fused else "gpsimd", "t")
                if k.fused:
                    wv_ = P.dma("sync", wslice, h["wout8"][fo, c],
                                stream="wo7" + str(c % 2))
                    P.wait("tensor", "dS_wo7" + str(c % 2), wv_)
                else:
                    wv_ = P.dma("gpsimd", wslice, h["wout"][fo, c],
                                stream="wo7" + str(c % 2))
                    P.wait("tensor", "dG_wo7" + str(c % 2), wv_)
                if c == 0:
                    P.wait("tensor", "v")
                for j8 in range(8):
                    j = jh * 8 + j8
                    if j8 < 6:
                        outap = k.ps[j8][:]
                    else:
                        outap = k.ps[j8][:].bitcast(F32)
                    tprev = P.op("tensor", "matmul", outap,
                         AFM[:, c * 2048 + j * 128: c * 2048 + (j + 1) * 128],
                         wslice, start=(c == 0), stop=(c == NCH - 1),
                         inc="t" if j8 == 7 else None)
            P.wait("vector", "t")
            for hv in range(2):
                P.wait("vector", "dG_o7")
                for j8 in range(hv * 4, hv * 4 + 4):
                    inap = k.ps[j8][:] if j8 < 6 else k.ps[j8][:].bitcast(F32)
                    P.op("vector", "tensor_copy",
                         OUT7[:, (j8 % 4) * 512:(j8 % 4 + 1) * 512],
                         inap, inc="v" if j8 % 4 == 3 else None)
                P.wait("gpsimd", "v")
                P.dma("gpsimd",
                      h["pre3"].rearrange("(tt p) f -> p tt f", p=128)[
                          :, jh * 8 + hv * 4: jh * 8 + (hv + 1) * 4,
                          fo * 512:(fo + 1) * 512],
                      OUT7[:, 0:2048].rearrange("p (tt d) -> p tt d", d=512),
                      stream="o7")
    P.barrier()


def emit_p8(k: K):
    if not k.fused:
        emit_fm(k, "x3", k.s["H8"][:, 0:2048])
    if k.fused and k.pgw is not None and k.pgw.cv8 is not None:
        k.P.wait("sync", "dG_cv8", k.pgw.cv8)
    P, h, s, ps = k.P, k.h, k.s, k.ps
    RFM, H8 = s["RFM"], s["H8"]
    WGB5, WUB5, WD5 = s["WGB5"], s["WUB5"], s["WD5"]
    SG5, P5E = s["SG5"], s["P5E"]
    GS8, US8, DSB = s["GS8"], s["US8"], s["DSB"]
    DS1 = s["P5E"][0:1, 0:2048]
    onesrf = s["onesrf"]
    PS_G5, PS_U5, PS_D0, PS_D1, PS_R = ps[0], ps[1], ps[2], ps[3], ps[4]
    P.op("vector", "memset", onesrf[:], 1.0, inc="v")
    P.dma("sync", GS8[:], h["gs"], stream="c")
    P.dma("sync", US8[:], h["us"], stream="c")
    cd = P.dma("sync", DS1, h["dsrow"], stream="c")
    P.wait("tensor", "dS_c", cd)
    P.wait("tensor", "v")
    for tb in range(4):
        P.op("tensor", "matmul", PS_R[:], onesrf[:],
             DS1[0:1, tb * 512:(tb + 1) * 512], start=True, stop=True, inc="t")
        P.wait("vector", "t")
        P.op("vector", "tensor_copy", DSB[:, tb * 512:(tb + 1) * 512], PS_R[:],
             inc="v")
        P.wait("tensor", "v")
    P.wait("scalar", "dS_c", cd)
    for th in range(2):
        tmark = {}
        for ot in range(NI1):
            hf = ot % 2
            if ot >= 2:
                P.wait("sync" if k.fused else "gpsimd", "t", tmark[ot - 2])
            if k.fused:
                P.dma("sync", WGB5[:, hf * 2048:(hf + 1) * 2048], h["gw8"][ot],
                      stream="wg8" + str(hf))
                wv_ = P.dma("sync", WUB5[:, hf * 2048:(hf + 1) * 2048],
                            h["uw8"][ot], stream="wg8" + str(hf))
                P.wait("tensor", "dS_wg8" + str(hf), wv_)
            else:
                P.dma("gpsimd", WGB5[:, hf * 2048:(hf + 1) * 2048], h["gwT"][ot],
                      stream="wg8" + str(hf))
                wv_ = P.dma("gpsimd", WUB5[:, hf * 2048:(hf + 1) * 2048],
                            h["uwT"][ot], stream="wg8" + str(hf))
                P.wait("tensor", "dG_wg8" + str(hf), wv_)
            for tb2 in range(2):
                tc = th * 1024 + tb2 * 512
                par = tb2 % 2
                PS_G5 = k.ps[0] if par == 0 else k.ps[2]
                PS_U5 = k.ps[1] if par == 0 else k.ps[3]
                if ot == 0:
                    P.wait("tensor", "a")
                    P.wait("tensor", "v")
                else:
                    P.wait("tensor", "a", max(P.sems["a"].n - 1, 0) if "a" in P.sems else 0)
                    P.wait("tensor", "v", max(P.sems["v"].n - 1, 0) if "v" in P.sems else 0)
                for c in range(NCH):
                    P.op("tensor", "matmul", PS_G5[:],
                         WGB5[:, hf * 2048 + c * 128: hf * 2048 + (c + 1) * 128],
                         RFM[:, c * 2048 + tc: c * 2048 + tc + 512],
                         start=(c == 0), stop=(c == NCH - 1))
                for c in range(NCH):
                    tmark[ot] = P.op("tensor", "matmul", PS_U5[:],
                         WUB5[:, hf * 2048 + c * 128: hf * 2048 + (c + 1) * 128],
                         RFM[:, c * 2048 + tc: c * 2048 + tc + 512],
                         start=(c == 0), stop=(c == NCH - 1),
                         inc="t" if c == NCH - 1 else None)
                P.wait("scalar", "t")
                P.wait("scalar", "v")
                P.op("scalar", "activation", SG5[:], PS_G5[:], AF.Silu,
                     scale=GS8[:, ot:ot + 1], inc="a")
                P.wait("vector", "a")
                P.op("vector", "scalar_tensor_tensor",
                     H8[:, ot * 1024 + tb2 * 512: ot * 1024 + tb2 * 512 + 512],
                     PS_U5[:], US8[:, ot:ot + 1], SG5[:], ALU.mult, ALU.mult,
                     inc="v")
        evmk = {}
        for fg in range(4):
            tprev = None
            for ic in range(NI1):
                hf = ic % 2
                if tprev is not None:
                    P.wait("sync" if k.fused else "gpsimd", "t", tprev)
                else:
                    P.wait("sync" if k.fused else "gpsimd", "t")
                if k.fused:
                    wd_ = P.dma("sync", WD5[:, hf * 512:(hf + 1) * 512],
                                h["dw8"][ic][:, fg * 512:(fg + 1) * 512],
                                stream="wd8" + str(hf))
                    P.wait("tensor", "dS_wd8" + str(hf), wd_)
                else:
                    wd_ = P.dma("gpsimd", WD5[:, hf * 512:(hf + 1) * 512],
                                h["dwT"][ic][:, fg * 512:(fg + 1) * 512],
                                stream="wd8" + str(hf))
                    P.wait("tensor", "dG_wd8" + str(hf), wd_)
                for t8 in range(8):
                    if ic == 0 and (fg - 1, t8) in evmk:
                        P.wait("tensor", "v", evmk[(fg - 1, t8)])
                    elif ic == 0 and t8 == 0:
                        P.wait("tensor", "v")
                    outap = k.ps[t8][:] if t8 < 6 else k.ps[t8][:].bitcast(F32)
                    tprev = P.op("tensor", "matmul", outap,
                         H8[:, ic * 1024 + t8 * 128: ic * 1024 + (t8 + 1) * 128],
                         WD5[:, hf * 512:(hf + 1) * 512],
                         start=(ic == 0), stop=(ic == NI1 - 1),
                         inc="t" if t8 == 7 else None)
            P.wait("vector", "t")
            P.wait("vector", "dG_p8")
            for t8 in range(8):
                inap = k.ps[t8][:] if t8 < 6 else k.ps[t8][:].bitcast(F32)
                evmk[(fg, t8)] = P.op("vector", "tensor_mul",
                     P5E[:, t8 * 512:(t8 + 1) * 512],
                     inap, DSB[:, fg * 512:(fg + 1) * 512], inc="v")
            P.wait("gpsimd", "v")
            P.dma("gpsimd",
                  h["pre4"].rearrange("(tt p) f -> p tt f", p=128)[
                      :, th * 8:(th + 1) * 8, fg * 512:(fg + 1) * 512],
                  P5E[:].rearrange("p (tt d) -> p tt d", d=512), stream="p8")
    P.barrier()


def alloc_sbuf(nc, es, names, suffix=""):
    SPECS = {
        "XFM": ((128, NCH * 2048), BF16),
        "RFM": ((128, NCH * 2048), BF16),
        "XF32": ((128, 8192), F32),
        "XT": ((128, 2048), F32),
        "XBF": ((128, 2048), BF16),
        "WF": ((128, 4096), F32),
        "WBF": ((128, 4096), BF16),
        "QT": ((128, 2048), F32),
        "SQ": ((128, 2048), F32),
        
        "KVBF": ((128, 2048), BF16),
        
        "NORMS2": ((128, 256), F32),
        "identb": ((128, 128), BF16),
        "identf": ((128, 128), F32),
        "onesf": ((128, 1), F32),
        "bq": ((128, NCH), F32),
        "bk": ((128, NCH), F32),
        "bv": ((128, NCH), F32),
        "MASK": ((128, 256), F32),
        "MASKB": ((128, 256), BF16),
        "INVM": ((128, 256), F32),
        "RANK": ((128, 256), F32),
        "CMP": ((128, 256), BF16),
        "LOa": ((1, 16), F32), "LOb": ((1, 16), F32),
        "HIa": ((1, 16), F32), "HIb": ((1, 16), F32),
        "MID": ((1, 16), F32), "MIDX": ((1, 256), F32),
        "CNT16": ((1, 16), F32), "GE": ((1, 16), I32), "TT16": ((1, 16), F32),
        "M1": ((128, 16), F32), "HROW": ((1, 16), F32), "H16C": ((16, 1), F32),
        "TOTS": ((1, 256), F32), "TOTJ": ((16, 16), F32),
        "OFFJS": ((16, 16), F32), "OFFROW": ((1, 256), F32),
        "PINCS": ((128, 256), F32),
        "onesb": ((128, 1), BF16), "onesrf": ((1, 128), F32),
        "triu": ((128, 128), BF16), "tris": ((16, 16), F32),
        "iota": ((128, SLACK), F32), "bemask": ((128, 256), F32),
        "QH": ((128, 4096), BF16), "KH": ((128, 4096), BF16),
        "VH": ((128, 4096), BF16), "QTM": ((128, 4096), BF16),
        "S2": ((128, 2 * NT * SLACK), BF16), "S2T": ((72, 2 * NT * 128), BF16),
        "PR": ((128, 768), BF16), "PT": ((128, 768), BF16),
        "CTXH": ((128, 4096), F32), "CTXO": ((128, 4096), F32),
        "MX": ((128, 2), F32), "NM": ((128, 2), F32), "RS": ((128, 2), F32),
        "SUMS": ((128, 2), F32),
        "QGB": ((128, SLACK), BF16), "GSF": ((72, 2048), F32),
        "GP": ((72, 2048), BF16), "GPT": ((128, NT * SLACK), BF16),
        "GCF": ((72, 128), BF16),
        "MXG": ((72, 1), F32), "NMG": ((72, 1), F32), "RSG": ((72, 1), F32),
        "GSUM": ((72, 1), F32),
        "XT4": ((128, 4096), F32), "CT4": ((128, 4096), F32),
        "SQ4": ((128, 4096), F32), "X1T": ((128, 4096), F32),
        "XB4": ((128, 2048), BF16),
        "SM": ((128, 2), F32), "VARS": ((128, 2), F32), "RSTD": ((128, 2), F32),
        "EPS4": ((128, 1), F32),
        "IDX5": ((128, 4096), I32), "IDXB": ((128, 4096), BF16),
        "TMP5": ((128, 2048), BF16), "WACC": ((128, 4096), BF16),
        "LUT1": ((1, 16), F32), "LUTB": ((128, 16), F32),
        "PGones": ((1, 128), F32), "PG1": ((128, 1), F32),
        "NEGL": ((128, 16), F32), "TMPA": ((128, 2048), BF16),
        "TMPB": ((128, 4096), BF16),
        "H5": ((128, NI0 * 1024), BF16),
        "H8": ((128, NI1 * 1024), BF16),
        "WGB5": ((128, 4096), BF16), "WUB5": ((128, 4096), BF16),
        "WD5": ((128, 1024), BF16),
        "SG5": ((128, 512), F32), "P5E": ((128, 4096), F32),
        "WB7": ((128, 4096), BF16),
        "QPH": ((128, 4096), BF16), "KPH": ((128, 4096), BF16),
        "VPH": ((128, 4096), BF16), "RKV": ((128, 4096), BF16),
        "RKS": ((128, 4096), BF16),
        "TMP7": ((128, 512), F32), "E7": ((128, 512), F32),
        "DEC1": ((1, 16), F32), "DECB": ((128, 16), F32),
        "DEN7": ((1, 2048), F32),
        "OUT7": ((128, 4096), F32),
        "GS8": ((128, NI1), F32), "US8": ((128, NI1), F32),
        "DSB": ((128, 2048), F32),
    }
    s = {}
    for name in names:
        shape, dt = SPECS[name]
        s[name] = es.enter_context(nc.sbuf_tensor("sb_" + name + suffix, list(shape), dt))
    return s


SBUF_BY_PHASE = {
    "p01": ["XFM", "XF32", "XT", "WF", "WBF", "QT", "SQ", "KVBF",
            "NORMS2", "identb", "identf", "onesf", "bq", "bk", "bv"],
    "p23": ["NORMS2", "identb", "identf", "onesf", "onesb", "onesrf",
            "triu", "tris", "iota", "bemask",
            "MASK", "MASKB", "INVM", "RANK", "CMP",
            "LOa", "LOb", "HIa", "HIb", "MID", "MIDX", "CNT16", "GE", "TT16",
            "M1", "HROW", "H16C", "TOTS", "TOTJ", "OFFJS", "OFFROW", "PINCS",
            "QH", "KH", "VH", "QTM", "S2", "S2T", "PR", "PT", "CTXH", "CTXO",
            "MX", "NM", "RS", "SUMS", "QGB", "GSF", "GP", "GPT", "GCF",
            "MXG", "NMG", "RSG", "GSUM"],
    "ln1": ["XT4", "CT4", "SQ4", "X1T", "SM", "VARS", "RSTD", "EPS4"],
    "ln2": ["XT4", "CT4", "SQ4", "X1T", "SM", "VARS", "RSTD", "EPS4"],
    "ln3": ["XT4", "CT4", "SQ4", "X1T", "SM", "VARS", "RSTD", "EPS4"],
    "ln4": ["XT4", "CT4", "SQ4", "X1T", "SM", "VARS", "RSTD", "EPS4"],
    "pgw": ["IDX5", "IDXB", "TMP5", "WACC", "LUT1", "LUTB", "PGones", "PG1", "NEGL", "TMPA", "TMPB"],
    "p5":  ["RFM", "XT4", "identb", "H5", "WGB5", "WUB5", "WD5",
            "SG5", "P5E"],
    "p7":  ["RFM", "XT4", "identb", "XFM", "WB7", "QPH", "KPH", "VPH",
            "RKV", "RKS", "TMP7", "E7", "DEC1", "DECB", "DEN7",
            "onesf", "onesrf", "onesb"],
    "p8":  ["RFM", "XT4", "identb", "H8", "WGB5", "WUB5", "WD5",
            "SG5", "P5E", "GS8", "US8", "DSB", "onesrf"],
}


def alloc_psum(nc, es):
    ps = []
    for i in range(6):
        ps.append(es.enter_context(nc.psum_tensor(f"ps{i}", [128, 512], F32)))
    for i in range(2):
        ps.append(es.enter_context(nc.psum_tensor(f"psb{i}", [128, 1024], BF16)))
    return ps


EMITTERS = {"p01": emit_p01, "p23": emit_p23,
            "ln1": emit_ln1, "ln2": emit_ln2, "ln3": emit_ln3, "ln4": emit_ln4,
            "pgw": emit_pgw, "p5": emit_p5, "p7": emit_p7, "p8": emit_p8}


def build_launch(phases, fused=False):
    nc = bass.Bass("TRN2", target_bir_lowering=False)
    reads, writes = set(), set()
    for ph in phases:
        r, w = PHASE_IO[ph]
        reads |= set(r)
        writes |= set(w)
    hbm, input_names = {}, []
    for name, (shape, dt, src_) in HBM_SPECS.items():
        if name not in reads and name not in writes:
            continue
        if src_ in ("in", "const") and name in reads:
            kind = "ExternalInput"
            input_names.append(name)
        elif name in writes:
            if fused:
                kind = "ExternalOutput" if src_ == "out" else "Internal"
            else:
                kind = "ExternalOutput"
        else:
            kind = "ExternalInput"
            input_names.append(name)
        hbm[name] = nc.dram_tensor(name, list(shape), dt, kind=kind).ap()

    PERSIST = ("identb", "identf", "onesf", "onesb", "onesrf")
    with ExitStack() as es:
        ps = alloc_psum(nc, es)
        P = Prog(nc)
        if fused:
            persist_names = []
            for ph in phases:
                for n in SBUF_BY_PHASE[ph]:
                    if n in PERSIST and n not in persist_names:
                        persist_names.append(n)
            has_pgw = "pgw" in phases
            sbp = alloc_sbuf(nc, es, persist_names)
            pgwobj = None
            es_pgw = None
            if has_pgw:
                es_pgw = ExitStack()
                sb_pgw = alloc_sbuf(nc, es_pgw, SBUF_BY_PHASE["pgw"],
                                    suffix="_pgw")
                kp = K(nc, P, hbm, {**sbp, **sb_pgw}, ps)
                kp.fused = True
                pgwobj = Pgw(kp)
            for i, ph in enumerate(phases):
                with ExitStack() as es2:
                    names = [n for n in SBUF_BY_PHASE[ph]
                             if n not in PERSIST and n not in sbp]
                    if ph in ("p5", "p8"):
                        names = [n for n in names if n != "XT4"]
                    sb = dict(alloc_sbuf(nc, es2, names, suffix=f"_{i}"))
                    sb.update(sbp)
                    k = K(nc, P, hbm, sb, ps)
                    k.fused = True
                    k.pgw = pgwobj
                    EMITTERS[ph](k)
                if ph == "p23" and es_pgw is not None:
                    es_pgw.close()
                    es_pgw = None
                if ph == "p23" and fused:
                    sbp.update(alloc_sbuf(nc, es, ["RFM"], suffix="_r"))
            if es_pgw is not None:
                es_pgw.close()
        else:
            sb_names = []
            for ph in phases:
                for n in SBUF_BY_PHASE[ph]:
                    if n not in sb_names:
                        sb_names.append(n)
            sb = alloc_sbuf(nc, es, sb_names)
            k = K(nc, P, hbm, sb, ps)
            for ph in phases:
                EMITTERS[ph](k)
        for n in list(P.sems):
            if n.startswith("d") and "_" in n:
                P.wait("gpsimd", n, P.sems[n].n)
        block = es.enter_context(nc.Block())
        P.replay(block)
    return nc, input_names


def run_groups(shared, xs, groups, n_cores=8):
    results = [dict() for _ in range(n_cores)]
    nlaunch = 0
    for group in groups:
        group = [p for p in group if p in EMITTERS]
        if not group:
            continue
        nc, input_names = build_launch(group)
        in_maps = []
        for b in range(n_cores):
            m = {}
            for name in input_names:
                if name == "x":
                    m[name] = xs[b]
                elif name in shared:
                    m[name] = shared[name]
                else:
                    m[name] = results[b][name]
            in_maps.append(m)
        res = run_bass_kernel_spmd(nc, in_maps, core_ids=list(range(n_cores)))
        nlaunch += 1
        for b in range(n_cores):
            for kk, v in res.results[b].items():
                results[b][kk] = np.asarray(v)
    return results


ALL_PHASES = ["pgw", "p01", "p23", "ln1", "p5", "ln2", "p7", "ln3", "p8", "ln4"]


def kernel(**inputs):
    shared, xs = host_prepare(inputs)
    nc, input_names = build_launch(ALL_PHASES, fused=True)
    in_maps = []
    for b in range(8):
        m = {}
        for name in input_names:
            m[name] = xs[b] if name == "x" else shared[name]
        in_maps.append(m)
    res = run_bass_kernel_spmd(nc, in_maps, core_ids=list(range(8)))
    return np.stack([np.asarray(res.results[b]["y"]) for b in range(8)], axis=0)
